# version 14
# baseline (speedup 1.0000x reference)
import math
from contextlib import ExitStack

import numpy as np
import ml_dtypes
import concourse.bass as bass
import concourse.mybir as mybir
from concourse.bass_utils import run_bass_kernel_spmd

F32 = mybir.dt.float32
BF16 = mybir.dt.bfloat16
I32 = mybir.dt.int32
AF = mybir.ActivationFunctionType
ALU = mybir.AluOpType
AX = mybir.AxisListType

D = 4096
SEQ = 2048
NB = 4
HH = 16
OH = HH * 128
NKT = D // 128
NQT = SEQ // 128
INTER = 11008
NIT = INTER // 128
MROWS = 1024
SCALE = 1.0 / math.sqrt(128.0)
NEG = -1.0e9

GW_G = [2048, 1536, 1024, 512]
PT_GOFF = [0, 8192, 14336, 18432]
NGU = NIT * 4
GU_S0 = 8

ENGS = ("pe", "dve", "act", "pool", "sp")
ENG_SEMS = {"PE", "DV", "AC", "CC", "PL"}
SEM_NAMES = (["C0", "X0", "X1", "W0", "W1", "SK0", "SK1", "SV0", "SV1",
              "SG0", "SG1", "K0", "K1", "SA0", "SA1", "GA", "B0",
              "XG0", "XG1", "O0", "O1", "RB0", "RB1", "LS0", "LS1",
              "XD0", "XD1", "XF0", "SI0", "SI1",
              "UW0", "UW1", "IN0", "IN1", "IN2", "IN3"]
             + [f"WO{i}" for i in range(12)]
             + ["PE", "DV", "AC", "CC", "PL"])


class Plan:
    def __init__(self, sems):
        self.sems = sems
        self.ops = {e: [] for e in ENGS}
        self.cnt = {name: 0 for name in sems}

    def emit(self, eng, fn, waits=(), inc=None):
        sems = self.sems
        waits = list(waits)
        if eng == "dve":
            waits.append(("DV", self.cnt["DV"]))
            if inc is None:
                inc = "DV"
        elif eng == "act":
            waits.append(("AC", self.cnt["AC"]))
            if inc is None:
                inc = "AC"
        wlist = [(sems[s], v) for (s, v) in waits if v > 0]
        if inc is not None:
            by = 1 if inc in ENG_SEMS else 16
            self.cnt[inc] += by
            isem, iby = sems[inc], by
        else:
            isem, iby = None, 0

        def run(e, fn=fn, wlist=wlist, isem=isem, iby=iby):
            for s, v in wlist:
                e.wait_ge(s, v)
            inst = fn(e)
            if isem is not None:
                inst.then_inc(isem, iby)

        self.ops[eng].append(run)
        return self.cnt[inc] if inc is not None else None

    def wait(self, eng, waits):
        sems = self.sems
        wlist = [(sems[s], v) for (s, v) in waits if v > 0]

        def run(e, wlist=wlist):
            for s, v in wlist:
                e.wait_ge(s, v)

        self.ops[eng].append(run)


def build_nc(upto=7, debug=False):
    nc = bass.Bass("TRN2", target_bir_lowering=False)
    dkind = "ExternalOutput" if debug else "Internal"

    xa = nc.dram_tensor("xa", [SEQ, D], F32, kind="ExternalInput").ap()
    xo = nc.dram_tensor("xo", [MROWS, D], F32, kind="ExternalInput").ap()
    wq = nc.dram_tensor("wq", [HH, 128, D], F32, kind="ExternalInput").ap()
    wk = nc.dram_tensor("wk", [HH, 128, D], F32, kind="ExternalInput").ap()
    wv = nc.dram_tensor("wv", [HH, 128, D], F32, kind="ExternalInput").ap()
    wo = nc.dram_tensor("wo", [8, 128, 32 * 512], F32, kind="ExternalInput").ap()
    gwh = nc.dram_tensor("gwh", [NIT, 128, D], F32, kind="ExternalInput").ap()
    uwh = nc.dram_tensor("uwh", [NIT, 128, D], F32, kind="ExternalInput").ap()
    dwh = nc.dram_tensor("dwh", [32, 128, INTER], F32, kind="ExternalInput").ap()
    bqh = nc.dram_tensor("bqh", [128, HH], F32, kind="ExternalInput").ap()
    bkh = nc.dram_tensor("bkh", [128, HH], F32, kind="ExternalInput").ap()
    bvh = nc.dram_tensor("bvh", [128, HH], F32, kind="ExternalInput").ap()
    bo = nc.dram_tensor("bo", [1, D], F32, kind="ExternalInput").ap()
    gsc = nc.dram_tensor("gsc", [1, 1], F32, kind="ExternalInput").ap()
    usc = nc.dram_tensor("usc", [1, 1], F32, kind="ExternalInput").ap()
    dsc = nc.dram_tensor("dsc", [1, 1], F32, kind="ExternalInput").ap()
    identw = nc.dram_tensor("identw", [128, 128], BF16, kind="ExternalInput").ap()
    identfw = nc.dram_tensor("identfw", [128, 128], F32, kind="ExternalInput").ap()
    maskw = nc.dram_tensor("maskw", [128, 128], F32, kind="ExternalInput").ap()
    gidx = nc.dram_tensor("gidx", [128, 32], I32, kind="ExternalInput").ap()
    outp = nc.dram_tensor("outp", [MROWS, D], F32, kind="ExternalOutput").ap()

    K_s = nc.dram_tensor("K_s", [HH, 128, SEQ], BF16, kind=dkind).ap()
    Q_s = nc.dram_tensor("Q_s", [HH, 128, SEQ], BF16, kind=dkind).ap()
    V_s = nc.dram_tensor("V_s", [HH, 128, SEQ], BF16, kind=dkind).ap()
    cc_in = nc.dram_tensor("cc_in", [OH, SEQ], BF16, kind=dkind).ap()
    cc_out = nc.dram_tensor("cc_out", [2 * OH, SEQ], BF16, kind=dkind).ap()
    xn_d = nc.dram_tensor("xn_d", [D, SEQ], BF16, kind=dkind).ap()
    inter_d = nc.dram_tensor("inter_d", [NIT, 128, MROWS], BF16, kind=dkind).ap()
    g_s = nc.dram_tensor("g_s", [MROWS, D], F32, kind=dkind).ap()
    led_d = nc.dram_tensor("led_d", [16, 256], F32, kind="Internal").ap()
    xnd = (nc.dram_tensor("xnd", [128, 65536], BF16, kind="ExternalOutput").ap()
           if debug and upto <= 1 else None)

    with ExitStack() as stack:
        sems = {n: stack.enter_context(nc.semaphore(n)) for n in SEM_NAMES}

        AB = stack.enter_context(nc.sbuf_tensor("AB", [128, 82000], BF16))
        AFR = stack.enter_context(nc.sbuf_tensor("AFR", [128, 9728], F32))
        ident = stack.enter_context(nc.sbuf_tensor("ident", [128, 128], BF16))
        identf = stack.enter_context(nc.sbuf_tensor("identf", [128, 128], F32))
        maskt = stack.enter_context(nc.sbuf_tensor("maskt", [128, 128], F32))
        biasq = stack.enter_context(nc.sbuf_tensor("biasq", [128, HH], F32))
        biask = stack.enter_context(nc.sbuf_tensor("biask", [128, HH], F32))
        biasv = stack.enter_context(nc.sbuf_tensor("biasv", [128, HH], F32))
        svec = stack.enter_context(nc.sbuf_tensor("svec", [128, 8], F32))
        stat = stack.enter_context(nc.sbuf_tensor("stat", [128, 24], F32))
        gidx_sb = stack.enter_context(nc.sbuf_tensor("gidx_sb", [128, 32], I32))
        rrmat = stack.enter_context(nc.sbuf_tensor("rrmat", [128, 32], F32))
        led = stack.enter_context(nc.sbuf_tensor("led", [16, 256], F32))
        rr_war = [0]

        PSF = [stack.enter_context(nc.psum_tensor(f"psf{i}", [128, 512], F32))
               for i in range(6)]
        PSB = [stack.enter_context(nc.psum_tensor(f"psb{i}", [128, 1024], BF16))
               for i in range(2)]

        p = Plan(sems)
        c = p.cnt

        psf_free = [("PE", 0)] * 6
        psb_free = [("PE", 0)] * 2

        def ab(off, n):
            return AB[:, off:off + n]

        def af(off, n):
            return AFR[:, off:off + n]

        for dst_t, src_t in ((ident, identw), (identf, identfw),
                             (maskt, maskw), (biasq, bqh), (biask, bkh),
                             (biasv, bvh), (gidx_sb, gidx)):
            p.emit("sp", lambda e, d=dst_t, s=src_t: e.dma_start(
                out=d[:, :], in_=s), inc="C0")
        for col, src_t in ((0, gsc), (1, usc), (2, dsc)):
            p.emit("sp", lambda e, col=col, s=src_t: e.dma_start(
                out=svec[:, col:col + 1],
                in_=s.partition_broadcast(128)[:, 0, :]), inc="C0")
        consts_ld = c["C0"]
        for eng in ("pe", "dve", "act"):
            p.wait(eng, [("C0", consts_ld)])
        usds_dv = p.emit(
            "dve", lambda e: e.tensor_mul(svec[:, 3:4], svec[:, 1:2],
                                          svec[:, 2:3]), inc="DV")

        def ln_pass(src_ap, nrow_tiles, xn_off, mwidth, xbuf_off, xnrm_off):
            dv0, pe0 = c["DV"], c["PE"]
            xd_done = [0, 0]
            dv_free = [("DV", dv0), ("DV", dv0)]
            nrm_free = [dv0, dv0]
            nrm_pe_free = [pe0, pe0]
            cks = []
            for t in range(nrow_tiles):
                sl = t % 2
                sb = 16 * sl
                xsem = f"X{sl}"
                xb = af(xbuf_off + sl * 4096, 4096)
                xd_done[sl] = p.emit(
                    "sp",
                    lambda e, src=src_ap, t=t, xb=xb: e.dma_start(
                        out=xb, in_=src[t * 128:(t + 1) * 128, :]),
                    waits=[dv_free[sl]], inc=xsem)
                xr = ab(xnrm_off + sl * 4096, 4096)
                p.emit("dve", lambda e, xb=xb, sb=sb: e.reduce_sum(
                    stat[:, sb + 0:sb + 1], xb, AX.X),
                    waits=[(xsem, xd_done[sl])])
                ac_sq = p.emit("act", lambda e, xb=xb, xr=xr: e.activation(
                    xr, xb, AF.Square),
                    waits=[(xsem, xd_done[sl]), ("DV", nrm_free[sl])], inc="AC")
                p.emit("dve", lambda e, xr=xr, sb=sb: e.reduce_sum(
                    stat[:, sb + 1:sb + 2], xr, AX.X), waits=[("AC", ac_sq)])
                p.emit("dve", lambda e, sb=sb: e.tensor_scalar(
                    stat[:, sb + 2:sb + 3], stat[:, sb + 0:sb + 1],
                    1.0 / D, None, ALU.mult))
                p.emit("dve", lambda e, sb=sb: e.tensor_mul(
                    stat[:, sb + 3:sb + 4], stat[:, sb + 2:sb + 3],
                    stat[:, sb + 2:sb + 3]))
                p.emit("dve", lambda e, sb=sb: e.tensor_scalar(
                    stat[:, sb + 4:sb + 5], stat[:, sb + 1:sb + 2],
                    1.0 / D, 1.0e-5, ALU.mult, ALU.add))
                dv_var = p.emit("dve", lambda e, sb=sb: e.tensor_sub(
                    stat[:, sb + 4:sb + 5], stat[:, sb + 4:sb + 5],
                    stat[:, sb + 3:sb + 4]), inc="DV")
                ac_std = p.emit("act", lambda e, sb=sb: e.activation(
                    stat[:, sb + 5:sb + 6], stat[:, sb + 4:sb + 5], AF.Sqrt),
                    waits=[("DV", dv_var)], inc="AC")
                p.emit("dve", lambda e, sb=sb: e.reciprocal(
                    stat[:, sb + 6:sb + 7], stat[:, sb + 5:sb + 6]),
                    waits=[("AC", ac_std)])
                p.emit("dve", lambda e, sb=sb: e.tensor_mul(
                    stat[:, sb + 7:sb + 8], stat[:, sb + 2:sb + 3],
                    stat[:, sb + 6:sb + 7]))
                dv_nmr = p.emit("dve", lambda e, sb=sb: e.tensor_scalar(
                    stat[:, sb + 7:sb + 8], stat[:, sb + 7:sb + 8],
                    -1.0, None, ALU.mult), inc="DV")
                ac_xn = p.emit("act", lambda e, xb=xb, xr=xr, sb=sb:
                               e.activation(
                                   xr, xb, AF.Identity,
                                   bias=stat[:, sb + 7:sb + 8],
                                   scale=stat[:, sb + 6:sb + 7]),
                               waits=[("DV", dv_nmr),
                                      ("PE", nrm_pe_free[sl])], inc="AC")
                dv_free[sl] = ("AC", ac_xn)
                for g in range(4):
                    bank = g % 2
                    pe_tr = None
                    for j in range(8):
                        kt = g * 8 + j
                        pe_tr = p.emit(
                            "pe",
                            lambda e, xr=xr, kt=kt, j=j, bank=bank: e.transpose(
                                PSB[bank][:, j * 128:(j + 1) * 128],
                                xr[:, kt * 128:(kt + 1) * 128], ident[:, :]),
                            waits=([("AC", ac_xn), psb_free[bank]]
                                   if j == 0 else []),
                            inc=("PE" if j == 7 else None))
                    base = xn_off + (g * 8) * mwidth + t * 128

                    def evac(e, bank=bank, base=base, mw=mwidth):
                        dst = AB[:, base:base + 8 * mw]
                        v = dst.rearrange("p (k m) -> p k m", k=8)[:, :, 0:128]
                        src = PSB[bank][:, :].rearrange("p (k m) -> p k m", k=8)
                        return e.tensor_copy(v, src)
                    dv_ev = p.emit("dve", evac, waits=[("PE", pe_tr)], inc="DV")
                    psb_free[bank] = ("DV", dv_ev)
                nrm_free[sl] = c["DV"]
                nrm_pe_free[sl] = c["PE"]
                if t % 4 == 3:
                    cks.append(c["DV"])
            return cks

        def finish():
            block = stack.enter_context(nc.Block())
            block.tensor(lambda e: [f(e) for f in p.ops["pe"]])
            block.vector(lambda e: [f(e) for f in p.ops["dve"]])
            block.scalar(lambda e: [f(e) for f in p.ops["act"]])
            block.gpsimd(lambda e: [f(e) for f in p.ops["pool"]])
            block.sync(lambda e: [f(e) for f in p.ops["sp"]])
            return nc

        XN_OFF = 0
        XNRM_OFF = 65536
        WBUF_OFF = 73728
        KQSTAGE_OFF = 65536
        VFM_OFF = 66560
        VRM_OFF = 67584
        p1_cks = ln_pass(xa, NQT, XN_OFF, SEQ, 0, XNRM_OFF)
        p1_dv = p1_cks[-1]
        if upto <= 1:
            if debug:
                xn_st = p.emit("sp", lambda e: e.dma_start(
                    out=xnd, in_=AB[:, 0:65536]), waits=[("DV", p1_dv)],
                    inc="GA")
                p.wait("sp", [("GA", xn_st)])
            return finish()

        wd_done = [0, 0]
        wfree = [0, 0]
        kq_free = [0, 0]
        vfm_free = [0, 0]
        vrm_free = [0, 0]
        mats = [(wk, biask, K_s), (wv, biasv, V_s), (wq, biasq, Q_s)]
        for ot in range(HH):
            if 8 <= ot < 16:
                for j2 in range(4):
                    kt = (ot - 8) * 4 + j2
                    ssl = kt % 2
                    p.emit("sp", lambda e, kt=kt: e.dma_start(
                        out=xn_d[kt * 128:(kt + 1) * 128, :],
                        in_=ab(XN_OFF + kt * SEQ, SEQ)),
                        waits=([("DV", p1_dv)] if kt == 0 else []),
                        inc=f"XD{ssl}")
            for mi, (wmat, bvec, dst) in enumerate(mats):
                it = ot * 3 + mi
                sl = it % 2
                wsem = f"W{sl}"
                wb = ab(WBUF_OFF + sl * 4096, 4096)
                wd_done[sl] = p.emit(
                    "pool", lambda e, wmat=wmat, ot=ot, wb=wb: e.dma_start(
                        out=wb, in_=wmat[ot]),
                    waits=[("PE", wfree[sl])], inc=wsem)
                vsl = ot % 2
                vr = ab(VRM_OFF + vsl * 2048, 2048)
                v_dv = [None] * 4

                def emit_vtrans(mc):
                    ssl = mc % 2
                    vf = ab(VFM_OFF + ssl * 512, 512)
                    bank = mc % 2
                    pe_tr = None
                    for ms in range(4):
                        pe_tr = p.emit(
                            "pe",
                            lambda e, vf=vf, ms=ms, bank=bank: e.transpose(
                                PSB[bank][:, ms * 128:(ms + 1) * 128],
                                vf[:, ms * 128:(ms + 1) * 128],
                                ident[:, :]),
                            waits=([v_dv[mc], psb_free[bank]]
                                   if ms == 0 else []),
                            inc=("PE" if ms == 3 else None))
                    vfm_free[ssl] = pe_tr
                    dv_ev = p.emit(
                        "dve",
                        lambda e, vr=vr, mc=mc, bank=bank: e.tensor_copy(
                            vr[:, mc * 512:(mc + 1) * 512],
                            PSB[bank][:, 0:512]),
                        waits=([("PE", pe_tr), (f"SV{vsl}", vrm_free[vsl])]
                               if mc == 0 else [("PE", pe_tr)]),
                        inc="DV")
                    psb_free[bank] = ("DV", dv_ev)

                for mc in range(4):
                    for kt in range(NKT):
                        wl = []
                        if kt == 0:
                            wl.append(psf_free[mc])
                            if mc == 0:
                                wl.append((wsem, wd_done[sl]))
                            if it == 0:
                                wl.append(("DV", p1_cks[mc]))
                        p.emit(
                            "pe",
                            lambda e, wb=wb, kt=kt, mc=mc: e.matmul(
                                PSF[mc][:, :],
                                wb[:, kt * 128:(kt + 1) * 128],
                                ab(XN_OFF + kt * SEQ + mc * 512, 512),
                                start=(kt == 0), stop=(kt == NKT - 1),
                                skip_group_check=True),
                            waits=wl,
                            inc=("PE" if kt == NKT - 1 else None))
                    pe_mc = c["PE"]
                    if mi != 1:
                        ssl = mc % 2
                        st_ = ab(KQSTAGE_OFF + ssl * 512, 512)
                        dv_e = p.emit(
                            "dve",
                            lambda e, mc=mc, st_=st_, bvec=bvec, ot=ot:
                            e.tensor_scalar(st_, PSF[mc][:, :],
                                            bvec[:, ot:ot + 1], None,
                                            ALU.add),
                            waits=[("PE", pe_mc),
                                   (f"SK{ssl}", kq_free[ssl])],
                            inc="DV")
                        psf_free[mc] = ("DV", dv_e)
                        kq_free[ssl] = p.emit(
                            "sp",
                            lambda e, dst=dst, ot=ot, mc=mc, st_=st_:
                            e.dma_start(
                                out=dst[ot][:, mc * 512:(mc + 1) * 512],
                                in_=st_),
                            waits=[("DV", dv_e)], inc=f"SK{ssl}")
                    else:
                        ssl = mc % 2
                        vf = ab(VFM_OFF + ssl * 512, 512)
                        dv_e = p.emit(
                            "dve",
                            lambda e, mc=mc, vf=vf, ot=ot: e.tensor_scalar(
                                vf, PSF[mc][:, :], biasv[:, ot:ot + 1],
                                None, ALU.add),
                            waits=[("PE", pe_mc), ("PE", vfm_free[ssl])],
                            inc="DV")
                        psf_free[mc] = ("DV", dv_e)
                        v_dv[mc] = ("DV", dv_e)
                        if mc > 0:
                            emit_vtrans(mc - 1)
                if mi == 1:
                    emit_vtrans(3)
                    vrm_free[vsl] = p.emit(
                        "sp",
                        lambda e, ot=ot, vr=vr: e.dma_start(
                            out=V_s[ot], in_=vr),
                        waits=[("DV", c["DV"])], inc=f"SV{vsl}")
                wfree[sl] = c["PE"]
        p2_stores = [("SK0", c["SK0"]), ("SK1", c["SK1"]),
                     ("SV0", c["SV0"]), ("SV1", c["SV1"])]
        p2_pe = c["PE"]
        p2sk = (c["SK0"], c["SK1"])
        xd_all = [("XD0", c["XD0"]), ("XD1", c["XD1"])]
        if upto <= 2:
            p.wait("sp", p2_stores + xd_all)
            return finish()

        XNF_OFF = 0
        GWB_OFF = 32768
        UWB_OFF = 36864
        ISTG_OFF = 40960
        KB_OFF = 43008
        VB_OFF = 47104
        QB_OFF = 51200
        PR_OFF = 55296
        PT_OFF = 59392
        AT_OFF = 79872
        RB_OFF = 8192
        xn_g = xn_d.rearrange("d (h m) -> (d h) m", h=2)
        for t in range(32):
            wl = ([("PE", p2_pe)] + xd_all) if t == 0 else []
            p.emit(
                "pool", lambda e, t=t: e.indirect_dma_start(
                    out=ab(XNF_OFF + t * 1024, 1024),
                    out_offset=None,
                    in_=xn_g,
                    in_offset=bass.IndirectOffsetOnAxis(
                        gidx_sb[:, t:t + 1], 0)),
                waits=wl, inc="XF0")
        xf_all = c["XF0"]

        gwc = {}
        uwc = {}
        gu = {"next": 0, "gw_free": p2_pe, "uw_free": p2_pe,
              "g0_pe": 0, "sil_ac": 0}
        ist_free = [0, 0]
        first_mul = [True]

        def emit_gw_load(it):
            sl = it % 2
            wl = ([("PE", p2_pe)] + xd_all if it == 0
                  else [("PE", gu["gw_free"])])
            p.emit("pool", lambda e, it=it: e.dma_start(
                out=ab(GWB_OFF, 4096), in_=gwh[it]), waits=wl, inc=f"W{sl}")
            gwc[it] = c[f"W{sl}"]

        def emit_uw_load(it):
            sl = it % 2
            wl = ([("PE", p2_pe)] + xd_all if it == 0
                  else [("PE", gu["uw_free"])])
            p.emit("pool", lambda e, it=it: e.dma_start(
                out=ab(UWB_OFF, 4096), in_=uwh[it]), waits=wl, inc=f"UW{sl}")
            uwc[it] = c[f"UW{sl}"]

        def emit_gu_group(gi):
            it, j = gi // 4, gi % 4
            sl = it % 2
            bk = 2 + (j % 2)
            woff = GWB_OFF if j < 2 else UWB_OFF
            mc = j % 2
            wl0 = [psf_free[bk]]
            if j == 0:
                wl0.append((f"W{sl}", gwc[it]))
                if it == 0:
                    wl0.append(("XF0", xf_all))
            elif j == 2:
                wl0.append((f"UW{sl}", uwc[it]))
            for kt in range(NKT):
                p.emit(
                    "pe", lambda e, kt=kt, mc=mc, bk=bk, woff=woff: e.matmul(
                        PSF[bk][:, :],
                        AB[:, woff + kt * 128:woff + (kt + 1) * 128],
                        ab(XNF_OFF + kt * MROWS + mc * 512, 512),
                        start=(kt == 0), stop=(kt == NKT - 1),
                        skip_group_check=True),
                    waits=(wl0 if kt == 0 else []),
                    inc=("PE" if kt == NKT - 1 else None))
            gpe = c["PE"]
            gu["end_pe"] = gpe
            if j == 0:
                gu["g0_pe"] = gpe
            elif j == 1:
                gu["gw_free"] = gpe
                if it + 1 < NIT:
                    emit_gw_load(it + 1)
                for mc2, bkk in ((0, 2), (1, 3)):
                    dv1 = p.emit(
                        "dve", lambda e, mc2=mc2, bkk=bkk: e.tensor_scalar(
                            af(4096 + mc2 * 512, 512), PSF[bkk][:, :],
                            svec[:, 0:1], None, ALU.mult),
                        waits=[("PE", gpe if mc2 else gu["g0_pe"]),
                               ("DV", usds_dv), ("AC", gu["sil_ac"])],
                        inc="DV")
                    psf_free[bkk] = ("DV", dv1)
                gu["sil_ac"] = p.emit(
                    "act", lambda e: e.activation(
                        af(5120, 1024), af(4096, 1024), AF.Silu),
                    waits=[("DV", c["DV"])], inc="AC")
            else:
                mc2 = j - 2
                wlm = [("AC", gu["sil_ac"]), ("PE", gpe),
                       (f"SI{sl}", ist_free[sl])]
                if first_mul[0]:
                    wlm += xd_all
                    first_mul[0] = False
                dv2 = p.emit(
                    "dve", lambda e, it=it, mc2=mc2, bk=bk: e.tensor_mul(
                        ab(ISTG_OFF + (it % 2) * 1024 + mc2 * 512, 512),
                        af(5120 + mc2 * 512, 512), PSF[bk][:, :]),
                    waits=wlm, inc="DV")
                psf_free[bk] = ("DV", dv2)
                if j == 3:
                    gu["uw_free"] = gpe
                    if it + 1 < NIT:
                        emit_uw_load(it + 1)
                    dvm = c["DV"]
                    ist_free[sl] = p.emit(
                        "pool", lambda e, it=it: e.dma_start(
                            out=inter_d[it],
                            in_=ab(ISTG_OFF + (it % 2) * 1024, 1024)),
                        waits=[("DV", dvm)], inc=f"SI{sl}")

        emit_gw_load(0)
        emit_uw_load(0)

        def pt_col(t, q):
            g = t // 4
            return PT_OFF + PT_GOFF[g] + (t % 4) * GW_G[g] + (q - 4 * g) * 128

        kv_free = [p2_pe, p2_pe]
        at_free = [0, 0]
        pr_free = [0, 0]
        sc_free = [("AC", 0), ("AC", 0)]
        rb_free = [0, 0]
        led_st = [0, 0]
        rr_tr = [0, 0]
        prev_pt = ("DV", 0)
        prev_vb = None
        led_dv = [0, 0]
        first_pt = [True]
        first_exp = [True]
        p.wait("sp", p2_stores)

        def emit_pv(h, vb, pt_mark):
            for mc in range(4):
                bk = 4 + mc % 2
                ntk = 4 * mc + 4
                rsl = mc % 2
                rb = af(RB_OFF + rsl * 512, 512)
                rb3 = rb.rearrange("p (j c) -> p j c", j=4)
                rb_ld = p.emit(
                    "sp", lambda e, h=h, mc=mc, rb3=rb3: e.dma_start(
                        out=rb3,
                        in_=led_d[4 * mc:4 * mc + 4,
                                  (h % 2) * 128:(h % 2 + 1) * 128]
                        .partition_broadcast(128)),
                    waits=[(f"LS{h % 2}", led_st[h % 2]), ("DV", rb_free[rsl])],
                    inc=f"RB{rsl}")
                pe_pv = None
                for t in range(ntk):
                    off = max(0, t * 128 - mc * 512)
                    nn = 512 - off
                    wl = []
                    if t == 0:
                        wl.append(pt_mark)
                        wl.append(psf_free[bk])
                    pe_pv = p.emit(
                        "pe", lambda e, vb=vb, t=t, mc=mc, off=off, nn=nn,
                        bk=bk: e.matmul(
                            PSF[bk][:, off:off + nn],
                            vb[:, t * 128:(t + 1) * 128],
                            ab(pt_col(t, 4 * mc) + off, nn),
                            start=(t == 0), stop=(t == ntk - 1),
                            skip_group_check=True),
                        waits=wl, inc=("PE" if t == ntk - 1 else None))
                asl = mc % 2
                at = ab(AT_OFF + asl * 512, 512)
                dv_a = p.emit(
                    "dve", lambda e, at=at, bk=bk, rb=rb: e.tensor_mul(
                        at, PSF[bk][:, :], rb),
                    waits=[("PE", pe_pv), (f"RB{rsl}", rb_ld),
                           (f"SA{asl}", at_free[asl])],
                    inc="DV")
                psf_free[bk] = ("DV", dv_a)
                rb_free[rsl] = dv_a
                at_free[asl] = p.emit(
                    "sp", lambda e, h=h, mc=mc, at=at: e.dma_start(
                        out=cc_in[h * 128:(h + 1) * 128,
                                  mc * 512:(mc + 1) * 512], in_=at),
                    waits=[("DV", dv_a)], inc=f"SA{asl}")
            return c["PE"]

        ac_e_mark = {}
        bias_mark = {}
        for h in range(HH):
            sl = h % 2
            ksem = f"K{sl}"
            kb = ab(KB_OFF + sl * 2048, 2048)
            vb = ab(VB_OFF + sl * 2048, 2048)
            qb = ab(QB_OFF + sl * 2048, 2048)
            wl0 = [("PE", kv_free[sl])]
            if h == 0:
                wl0 += p2_stores + xd_all
            p.emit("sp", lambda e, h=h, kb=kb: e.dma_start(
                out=kb, in_=K_s[h]), waits=wl0, inc=ksem)
            p.emit("sp", lambda e, h=h, vb=vb: e.dma_start(
                out=vb, in_=V_s[h]), inc=ksem)
            kd_done = p.emit("sp", lambda e, h=h, qb=qb: e.dma_start(
                out=qb, in_=Q_s[h]), inc=ksem)

            for i in range(NQT + 2):
                if i < NQT:
                    qt = i
                    Kt = 128 * (qt + 1)
                    nkc = (Kt + 511) // 512
                    psl = qt % 2
                    sc = af(psl * 2048, 2048)
                    sc_ac = []
                    for kc in range(nkc):
                        nn = min(512, Kt - kc * 512)
                        bk = kc % 2
                        wl = [psf_free[bk]]
                        if qt == 0 and kc == 0:
                            wl.append((ksem, kd_done))
                        pe_s = p.emit(
                            "pe", lambda e, qb=qb, kb=kb, qt=qt, kc=kc,
                            nn=nn, bk=bk: e.matmul(
                                PSF[bk][:, 0:nn],
                                qb[:, qt * 128:(qt + 1) * 128],
                                kb[:, kc * 512:kc * 512 + nn],
                                start=True, stop=True,
                                skip_group_check=True),
                            waits=wl, inc="PE")
                        wl2 = [("PE", pe_s)]
                        if kc <= 1:
                            wl2.append(sc_free[psl])
                        if kc % 2 == 0:
                            ev = p.emit(
                                "act", lambda e, sc=sc, kc=kc, nn=nn, bk=bk:
                                e.copy(sc[:, kc * 512:kc * 512 + nn],
                                       PSF[bk][:, 0:nn]),
                                waits=wl2, inc="AC")
                            psf_free[bk] = ("AC", ev)
                            sc_ac.append(("AC", ev))
                        else:
                            ev = p.emit(
                                "dve", lambda e, sc=sc, kc=kc, nn=nn, bk=bk:
                                e.tensor_copy(sc[:, kc * 512:kc * 512 + nn],
                                              PSF[bk][:, 0:nn]),
                                waits=wl2, inc="DV")
                            psf_free[bk] = ("DV", ev)
                            sc_ac.append(("DV", ev))
                if i - 2 >= 0:
                    pqt = i - 2
                    pac = ac_e_mark[(h, pqt)]
                    ntile = pqt + 1
                    for g0 in range(0, ntile, 4):
                        g = g0 // 4
                        gsz = min(4, ntile - g0)
                        bank = g % 2
                        pe_tr = None
                        for j in range(gsz):
                            t = g0 + j
                            pe_tr = p.emit(
                                "pe", lambda e, pr2=ab(
                                    PR_OFF + (pqt % 2) * 2048, 2048), t=t,
                                j=j, bank=bank: e.transpose(
                                    PSB[bank][:, j * 128:(j + 1) * 128],
                                    pr2[:, t * 128:(t + 1) * 128],
                                    ident[:, :]),
                                waits=([("AC", pac), psb_free[bank]]
                                       if j == 0 else []),
                                inc=("PE" if j == gsz - 1 else None))
                        base = pt_col(4 * g, pqt)
                        gww = GW_G[g]

                        def evacp(e, base=base, gsz=gsz, bank=bank, gww=gww):
                            dst = AB[:, base:base + gsz * gww]
                            v = dst.rearrange("p (k m) -> p k m",
                                              k=gsz)[:, :, 0:128]
                            src = PSB[bank][:, 0:gsz * 128].rearrange(
                                "p (k m) -> p k m", k=gsz)
                            return e.tensor_copy(v, src)
                        wle = [("PE", pe_tr)]
                        if first_pt[0]:
                            wle += xd_all
                            first_pt[0] = False
                        dv_pt = p.emit("dve", evacp, waits=wle, inc="DV")
                        psb_free[bank] = ("DV", dv_pt)
                    pr_free[pqt % 2] = c["PE"]
                if h > 0 and i == 0:
                    kv_free[1 - sl] = emit_pv(h - 1, prev_vb, prev_pt)
                if i < NQT:
                    qt = i
                    Kt = 128 * (qt + 1)
                    psl = qt % 2
                    sc = af(psl * 2048, 2048)
                    dv_m = p.emit(
                        "dve", lambda e, sc=sc, Kt=Kt: e.tensor_add(
                            sc[:, Kt - 128:Kt], sc[:, Kt - 128:Kt],
                            maskt[:, :]),
                        waits=sc_ac, inc="DV")
                    p.emit("dve", lambda e, sc=sc, Kt=Kt, psl=psl:
                           e.reduce_max(stat[:, 8 + psl:9 + psl],
                                        sc[:, 0:Kt], AX.X, negate=True),
                           waits=[("DV", dv_m)])
                    bias_mark[(h, qt)] = p.emit(
                        "dve", lambda e, psl=psl: e.tensor_scalar(
                            stat[:, 10 + psl:11 + psl],
                            stat[:, 8 + psl:9 + psl], SCALE, None, ALU.mult),
                        waits=[("AC", ac_e_mark.get((h, qt - 2), 0))],
                        inc="DV")
                if i - 1 >= 0 and i - 1 < NQT:
                    qt = i - 1
                    Kt = 128 * (qt + 1)
                    psl = qt % 2
                    sc = af(psl * 2048, 2048)
                    pr = ab(PR_OFF + psl * 2048, 2048)
                    wle = [("DV", bias_mark[(h, qt)]),
                           ("PE", pr_free[psl]),
                           ("DV", rr_war[0])]
                    if first_exp[0]:
                        wle += xd_all
                        first_exp[0] = False
                    ac_e = p.emit(
                        "act", lambda e, sc=sc, pr=pr, Kt=Kt, psl=psl:
                        e.activation(
                            pr[:, 0:Kt], sc[:, 0:Kt], AF.Exp,
                            bias=stat[:, 10 + psl:11 + psl], scale=SCALE,
                            accum_out=stat[:, 12 + psl:13 + psl]),
                        waits=wle, inc="AC")
                    ac_e_mark[(h, qt)] = ac_e
                    sc_free[psl] = ("AC", ac_e)
                    rr_war[0] = p.emit(
                        "dve", lambda e, h=h, qt=qt, psl=psl: e.reciprocal(
                            rrmat[:, (h % 2) * 16 + qt:
                                  (h % 2) * 16 + qt + 1],
                            stat[:, 12 + psl:13 + psl]),
                        waits=[("AC", ac_e), ("PE", rr_tr[h % 2])],
                        inc="DV")
                if i < NQT:
                    s_glob = h * NQT + i
                    tgt = (0 if s_glob < GU_S0 else
                           min(NGU, (s_glob - GU_S0 + 1) * NGU
                               // (HH * NQT - GU_S0)))
                    while gu["next"] < tgt:
                        emit_gu_group(gu["next"])
                        gu["next"] += 1
            pe_rr = p.emit(
                "pe", lambda e, h=h: e.transpose(
                    PSF[3][0:16, 0:128],
                    rrmat[:, (h % 2) * 16:(h % 2) * 16 + 16],
                    identf[:, :]),
                waits=[("DV", rr_war[0]), psf_free[3]], inc="PE")
            rr_tr[h % 2] = pe_rr
            led_dv[h % 2] = p.emit(
                "dve", lambda e, h=h: e.tensor_copy(
                    led[0:16, (h % 2) * 128:(h % 2 + 1) * 128],
                    PSF[3][0:16, 0:128]),
                waits=[("PE", pe_rr), (f"LS{h % 2}", led_st[h % 2])],
                inc="DV")
            psf_free[3] = ("DV", led_dv[h % 2])
            led_st[h % 2] = p.emit(
                "sp", lambda e, h=h: e.dma_start(
                    out=led_d[:, (h % 2) * 128:(h % 2 + 1) * 128],
                    in_=led[0:16, (h % 2) * 128:(h % 2 + 1) * 128]),
                waits=[("DV", led_dv[h % 2])], inc=f"LS{h % 2}")
            prev_pt = ("DV", c["DV"])
            prev_vb = vb
        kv_free[(HH - 1) % 2] = emit_pv(HH - 1, prev_vb, prev_pt)
        while gu["next"] < NGU:
            emit_gu_group(gu["next"])
            gu["next"] += 1
        att_stores = [("SA0", c["SA0"]), ("SA1", c["SA1"])]
        si_all = [("SI0", c["SI0"]), ("SI1", c["SI1"])]
        p4_pe = c["PE"]
        cc_g = cc_out.rearrange("d (b m) -> (d b) m", b=2)
        if upto <= 4:
            p.wait("sp", att_stores)
            p.wait("pool", si_all + [("XF0", xf_all)])
            return finish()

        DW2_OFF = 44032

        def int_col(mch, il):
            if il < 30:
                return (66560 if mch == 0 else 28672) + il * 512
            return (il - 30) * 512

        def emit_int_loads(mch, il0, il1, wl_first):
            first = [True, True]
            for it in range(il0, il1):
                q = it % 2
                isl = q + 2 * mch
                wl = wl_first if first[q] else []
                first[q] = False
                p.emit("pool" if q == 0 else "sp",
                       lambda e, it=it, mch=mch: e.dma_start(
                           out=ab(int_col(mch, it), 512),
                           in_=inter_d[it][:, mch * 512:(mch + 1) * 512]),
                       waits=wl, inc=f"IN{isl}")

        def emit_dw_load(ot, sl2):
            p.emit("pool", lambda e, ot=ot, sl2=sl2: e.dma_start(
                out=AB[:, DW2_OFF + sl2 * INTER:
                       DW2_OFF + sl2 * INTER + INTER],
                in_=dwh[ot]),
                waits=[("PE", dw_free[sl2])], inc=f"W{sl2}")
            dwc[sl2] = c[f"W{sl2}"]

        dw_free = [p4_pe, p4_pe]
        dwc = [0, 0]
        fs_free = [("PE", 0), ("PE", 0)]
        gst_free = [0, 0]
        chunk_end = p4_pe
        cc_done = None
        ao_done1 = 0
        g_idx = 0
        for mch in range(2):
            if mch == 0:
                emit_int_loads(0, 30, NIT,
                               si_all + [("PE", gu["end_pe"])])
                emit_int_loads(0, 0, 30, [("PE", p4_pe)] + att_stores)
                in_tot = [("IN0", c["IN0"]), ("IN1", c["IN1"])]
                emit_dw_load(0, 0)
                emit_dw_load(1, 1)
                emit_int_loads(1, 0, 30, si_all + [("PE", p4_pe)])
                cc_done = p.emit(
                    "pool", lambda e: e.collective_compute(
                        "AllGather", ALU.bypass,
                        [[0, 1], [2, 3], [4, 5], [6, 7]],
                        [cc_in.opt()], [cc_out.opt()]),
                    waits=att_stores, inc="CC")
            else:
                emit_dw_load(0, 0)
                emit_int_loads(1, 30, NIT, [("PE", chunk_end)])
                emit_dw_load(1, 1)
                in_tot = [("IN2", c["IN2"]), ("IN3", c["IN3"])]
            for ot in range(32):
                sl2 = ot % 2
                if ot >= 2:
                    emit_dw_load(ot, sl2)
                if mch == 1 and ot == 8:
                    for t in range(15):
                        wl = ([("CC", cc_done), ("PE", chunk_end)]
                              if t == 0 else [])
                        p.emit(
                            "pool", lambda e, t=t: e.indirect_dma_start(
                                out=ab(66560 + t * 1024, 1024),
                                out_offset=None,
                                in_=cc_g,
                                in_offset=bass.IndirectOffsetOnAxis(
                                    gidx_sb[:, t:t + 1], 0)),
                            waits=wl, inc="GA")
                    ao_done1 = c["GA"]
                bk = (2 * g_idx) % 6
                bk2 = (2 * g_idx + 1) % 6
                g_idx += 1
                dwb = DW2_OFF + sl2 * INTER
                for il in range(NIT):
                    wl2 = []
                    if il == 0:
                        wl2 = [psf_free[bk], (f"W{sl2}", dwc[sl2])]
                        if ot == 0:
                            wl2 += in_tot
                    p.emit(
                        "pe", lambda e, il=il, dwb=dwb, mch=mch, bk=bk:
                        e.matmul(
                            PSF[bk][:, :],
                            AB[:, dwb + il * 128:dwb + (il + 1) * 128],
                            ab(int_col(mch, il), 512),
                            start=(il == 0), stop=(il == NIT - 1),
                            skip_group_check=True),
                        waits=wl2, inc=("PE" if il == NIT - 1 else None))
                pe_d = c["PE"]
                dw_free[sl2] = pe_d
                fs = af(sl2 * 512, 512)
                dv_fs = p.emit(
                    "dve", lambda e, fs=fs, bk=bk: e.tensor_scalar(
                        fs, PSF[bk][:, :], svec[:, 3:4], None, ALU.mult),
                    waits=[("PE", pe_d), ("DV", usds_dv), fs_free[sl2]],
                    inc="DV")
                psf_free[bk] = ("DV", dv_fs)
                pe_tr = None
                for j in range(4):
                    pe_tr = p.emit(
                        "pe", lambda e, fs=fs, j=j, bk2=bk2: e.transpose(
                            PSF[bk2][:, j * 128:(j + 1) * 128],
                            fs[:, j * 128:(j + 1) * 128], identf[:, :]),
                        waits=([("DV", dv_fs), psf_free[bk2]]
                               if j == 0 else []),
                        inc=("PE" if j == 3 else None))
                fs_free[sl2] = ("PE", pe_tr)
                gst = af(1024 + sl2 * 512, 512)
                ac_g = p.emit(
                    "act", lambda e, gst=gst, bk2=bk2: e.activation(
                        gst, PSF[bk2][:, :], AF.Silu),
                    waits=[("PE", pe_tr), (f"SG{sl2}", gst_free[sl2])],
                    inc="AC")
                psf_free[bk2] = ("AC", ac_g)
                gst_free[sl2] = p.emit(
                    "sp", lambda e, mch=mch, ot=ot, gst=gst: e.dma_start(
                        out=g_s.rearrange("(ms pp) d -> pp ms d", pp=128)
                        [:, 4 * mch:4 * mch + 4, ot * 128:(ot + 1) * 128],
                        in_=gst.rearrange("pp (ms oc) -> pp ms oc", ms=4)),
                    waits=[("AC", ac_g)], inc=f"SG{sl2}")
            chunk_end = c["PE"]
        down_end = c["PE"]
        g_stores = [("SG0", c["SG0"]), ("SG1", c["SG1"])]
        if upto <= 5:
            p.wait("sp", g_stores)
            p.wait("pool", [("CC", c["CC"]), ("GA", c["GA"])])
            return finish()

        WO_OFF = 17408

        def ao_col(dt):
            return 66560 + dt * 1024 if dt < 15 else (dt - 15) * 1024

        ao_all = None
        bo_ld = None
        NCHUNK = 12
        wo_done = [0] * NCHUNK
        wo_free = [down_end] * NCHUNK
        XC_OFF = 4096
        GC_OFF = 5120
        S1_OFF = 6144
        OT_OFF = 7168
        xg_free = [0, 0]
        s1_free = [0, 0]
        ot_free = [0, 0]
        pending_out = [None]
        chunk_id = 0
        for oc in range(8):
            for dg in range(8):
                ci = chunk_id % NCHUNK
                wo_done[ci] = p.emit(
                    "pool", lambda e, oc=oc, dg=dg, ci=ci: e.dma_start(
                        out=ab(WO_OFF + ci * 2048, 2048),
                        in_=wo[oc][:, dg * 2048:(dg + 1) * 2048]),
                    waits=[("PE", wo_free[ci])],
                    inc=f"WO{ci}")
                chunk_id += 1
                if oc == 0 and dg == 2:
                    for t in range(15, 32):
                        wl = ([("CC", cc_done), ("PE", down_end),
                               ("GA", ao_done1)] if t == 15 else [])
                        p.emit(
                            "pool", lambda e, t=t: e.indirect_dma_start(
                                out=ab(ao_col(t), 1024),
                                out_offset=None,
                                in_=cc_g,
                                in_offset=bass.IndirectOffsetOnAxis(
                                    gidx_sb[:, t:t + 1], 0)),
                            waits=wl, inc="GA")
                    ao_all = c["GA"]
            if oc == 0:
                bo_ld = p.emit("sp", lambda e: e.dma_start(
                    out=af(0, D), in_=bo.partition_broadcast(128)[:, 0, :]),
                    waits=[("GA", ao_all)] + g_stores, inc="B0")
            base_chunk = chunk_id - 8
            for gpi, mts in enumerate(((0, 1, 2), (3, 4, 5), (6, 7))):
                banks = {mt: (gpi * 3 + i) % 6 for i, mt in enumerate(mts)}
                for dt in range(32):
                    ci = (base_chunk + dt // 4) % NCHUNK
                    dl = dt % 4
                    for mt in mts:
                        bk = banks[mt]
                        wl = []
                        if dt == 0:
                            wl.append(psf_free[bk])
                            if mt == mts[0]:
                                wl.append(("GA", ao_done1))
                        if dt == 15 and mt == mts[0]:
                            wl.append(("GA", ao_all))
                        if dl == 0 and mt == mts[0]:
                            wl.append((f"WO{ci}", wo_done[ci]))
                        p.emit(
                            "pe", lambda e, dt=dt, mt=mt, ci=ci, dl=dl,
                            bk=bk: e.matmul(
                                PSF[bk][:, :],
                                ab(ao_col(dt) + mt * 128, 128),
                                ab(WO_OFF + ci * 2048 + dl * 512, 512),
                                start=(dt == 0), stop=(dt == 31),
                                skip_group_check=True),
                            waits=wl,
                            inc=("PE" if (dt == 31 and mt == mts[-1])
                                 else None))
                pe_done = c["PE"]
                if gpi == 2:
                    for k in range(8):
                        wo_free[(base_chunk + k) % NCHUNK] = pe_done
                for mt in mts:
                    bk = banks[mt]
                    sl2 = mt % 2
                    xgsem = f"XG{sl2}"
                    xc = af(XC_OFF + sl2 * 512, 512)
                    gc = af(GC_OFF + sl2 * 512, 512)
                    s1 = af(S1_OFF + sl2 * 512, 512)
                    ot_ = af(OT_OFF + sl2 * 512, 512)
                    p.emit(
                        "sp", lambda e, mt=mt, oc=oc, xc=xc: e.dma_start(
                            out=xc, in_=xo[mt * 128:(mt + 1) * 128,
                                           oc * 512:(oc + 1) * 512]),
                        waits=([("DV", xg_free[sl2])] +
                               ([("GA", ao_all), ("PE", down_end)]
                                if oc == 0 else [])),
                        inc=xgsem)
                    xg_ld = p.emit(
                        "sp", lambda e, mt=mt, oc=oc, gc=gc: e.dma_start(
                            out=gc, in_=g_s[mt * 128:(mt + 1) * 128,
                                            oc * 512:(oc + 1) * 512]),
                        waits=(g_stores if oc == 0 else []), inc=xgsem)
                    if pending_out[0] is not None:
                        pending_out[0]()
                        pending_out[0] = None
                    dv1 = p.emit(
                        "dve", lambda e, s1=s1, oc=oc, bk=bk: e.tensor_add(
                            s1, PSF[bk][:, :], af(oc * 512, 512)),
                        waits=[("PE", pe_done), ("B0", bo_ld),
                               ("DV", s1_free[sl2])], inc="DV")
                    psf_free[bk] = ("DV", dv1)
                    p.emit(
                        "dve", lambda e, s1=s1, gc=gc: e.tensor_mul(
                            s1, s1, gc),
                        waits=[(xgsem, xg_ld)], inc="DV")
                    dv3 = p.emit(
                        "dve", lambda e, ot_=ot_, s1=s1, xc=xc: e.tensor_add(
                            ot_, s1, xc),
                        waits=[(f"O{sl2}", ot_free[sl2])], inc="DV")
                    xg_free[sl2] = dv3
                    s1_free[sl2] = dv3

                    def do_out(mt=mt, oc=oc, ot_=ot_, dv3=dv3, sl2=sl2):
                        ot_free[sl2] = p.emit(
                            "sp", lambda e, mt=mt, oc=oc, ot_=ot_:
                            e.dma_start(
                                out=outp[mt * 128:(mt + 1) * 128,
                                         oc * 512:(oc + 1) * 512], in_=ot_),
                            waits=[("DV", dv3)], inc=f"O{sl2}")
                    pending_out[0] = do_out
        if pending_out[0] is not None:
            pending_out[0]()
            pending_out[0] = None
        p.wait("sp", [("O0", c["O0"]), ("O1", c["O1"])])
        return finish()


def _tile_w(wT, ntiles, width):
    K, N = wT.shape
    nkt = K // 128
    return np.ascontiguousarray(
        wT.reshape(nkt, 128, ntiles, width).transpose(2, 1, 0, 3).reshape(
            ntiles, 128, nkt * width))


def kernel(L0_attn_x, L0_attn_Wq, L0_attn_bq, L0_attn_Wk, L0_attn_bk,
           L0_attn_Wv, L0_attn_bv, L0_attn_Wo, L0_attn_bo,
           L0_ffn_gate_w, L0_ffn_up_w, L0_ffn_down_w,
           L0_ffn_gate_scale, L0_ffn_up_scale, L0_ffn_down_scale):
    x = np.asarray(L0_attn_x, dtype=np.float32)
    Wq = np.asarray(L0_attn_Wq, np.float32)
    Wk = np.asarray(L0_attn_Wk, np.float32)
    Wv = np.asarray(L0_attn_Wv, np.float32)
    Wo = np.asarray(L0_attn_Wo, np.float32)
    bq = np.asarray(L0_attn_bq, np.float32)
    bk = np.asarray(L0_attn_bk, np.float32)
    bv = np.asarray(L0_attn_bv, np.float32)
    bo_ = np.asarray(L0_attn_bo, np.float32)
    gw = np.asarray(L0_ffn_gate_w, np.float32)
    uw = np.asarray(L0_ffn_up_w, np.float32)
    dw = np.asarray(L0_ffn_down_w, np.float32)
    gs = np.float32(L0_ffn_gate_scale).reshape(1, 1)
    us = np.float32(L0_ffn_up_scale).reshape(1, 1)
    ds = np.float32(L0_ffn_down_scale).reshape(1, 1)

    ident = np.eye(128, dtype=ml_dtypes.bfloat16)
    identf = np.eye(128, dtype=np.float32)
    mask = np.triu(np.full((128, 128), NEG, np.float32), k=1)

    halves = []
    for r in range(2):
        sl = slice(OH * r, OH * (r + 1))
        wq_h = _tile_w(np.ascontiguousarray(Wq[sl, :].T), HH, 128)
        wk_h = _tile_w(np.ascontiguousarray(Wk[sl, :].T), HH, 128)
        wv_h = _tile_w(np.ascontiguousarray(Wv[sl, :].T), HH, 128)
        bq_h = np.ascontiguousarray(bq[sl].reshape(HH, 128).T)
        bk_h = np.ascontiguousarray(bk[sl].reshape(HH, 128).T)
        bv_h = np.ascontiguousarray(bv[sl].reshape(HH, 128).T)
        gidx_v = np.empty((128, 32), np.int32)
        for t in range(32):
            gidx_v[:, t] = (t * 128 + np.arange(128)) * 2 + r
        halves.append((wq_h, wk_h, wv_h, bq_h, bk_h, bv_h, gidx_v))

    wo_h = _tile_w(np.ascontiguousarray(Wo.T), 8, 512)
    gw_h = _tile_w(np.ascontiguousarray(gw.T), NIT, 128)
    uw_h = _tile_w(np.ascontiguousarray(uw.T), NIT, 128)
    dw_h = _tile_w(np.ascontiguousarray(dw.T), 32, 128)

    in_maps = []
    for cid in range(8):
        b = cid // 2
        r = cid % 2
        wq_h, wk_h, wv_h, bq_h, bk_h, bv_h, gidx_v = halves[r]
        in_maps.append({
            "xa": np.ascontiguousarray(x[b]),
            "xo": np.ascontiguousarray(x[b, MROWS * r:MROWS * (r + 1)]),
            "wq": wq_h, "wk": wk_h, "wv": wv_h, "wo": wo_h,
            "gwh": gw_h, "uwh": uw_h, "dwh": dw_h,
            "bqh": bq_h, "bkh": bk_h, "bvh": bv_h,
            "bo": bo_.reshape(1, D),
            "gsc": gs, "usc": us, "dsc": ds,
            "identw": ident, "identfw": identf, "maskw": mask,
            "gidx": gidx_v,
        })

    nc = build_nc()
    res = run_bass_kernel_spmd(nc, in_maps, core_ids=list(range(8)))
    out = np.empty((NB, SEQ, D), np.float32)
    for cid in range(8):
        b = cid // 2
        r = cid % 2
        out[b, MROWS * r:MROWS * (r + 1), :] = res.results[cid]["outp"]
    return out


# version 15
# speedup vs baseline: 1.0163x; 1.0163x over previous
import math
from contextlib import ExitStack

import numpy as np
import ml_dtypes
import concourse.bass as bass
import concourse.mybir as mybir
from concourse.bass_utils import run_bass_kernel_spmd

F32 = mybir.dt.float32
BF16 = mybir.dt.bfloat16
I32 = mybir.dt.int32
AF = mybir.ActivationFunctionType
ALU = mybir.AluOpType
AX = mybir.AxisListType

D = 4096
SEQ = 2048
NB = 4
HH = 16
OH = HH * 128
NKT = D // 128
NQT = SEQ // 128
INTER = 11008
NIT = INTER // 128
MROWS = 1024
SCALE = 1.0 / math.sqrt(128.0)
NEG = -1.0e9

GW_G = [2048, 1536, 1024, 512]
PT_GOFF = [0, 8192, 14336, 18432]
NGU = NIT * 4
GU_S0 = 8

ENGS = ("pe", "dve", "act", "pool", "sp")
ENG_SEMS = {"PE", "DV", "AC", "CC", "PL"}
SEM_NAMES = (["C0", "X0", "X1", "W0", "W1", "SK0", "SK1", "SV0", "SV1",
              "SG0", "SG1", "K0", "K1", "SA0", "SA1", "GA", "B0",
              "XG0", "XG1", "O0", "O1", "RB0", "RB1", "LS0", "LS1",
              "XD0", "XD1", "XF0", "SI0", "SI1",
              "UW0", "UW1", "IN0", "IN1", "IN2", "IN3",
              "DWC0", "DWS", "XP0", "XP1"]
             + [f"WO{i}" for i in range(12)]
             + ["PE", "DV", "AC", "CC"])


class Plan:
    def __init__(self, sems):
        self.sems = sems
        self.ops = {e: [] for e in ENGS}
        self.cnt = {name: 0 for name in sems}

    def emit(self, eng, fn, waits=(), inc=None):
        sems = self.sems
        waits = list(waits)
        if eng == "dve":
            waits.append(("DV", self.cnt["DV"]))
            if inc is None:
                inc = "DV"
        elif eng == "act":
            waits.append(("AC", self.cnt["AC"]))
            if inc is None:
                inc = "AC"
        wlist = [(sems[s], v) for (s, v) in waits if v > 0]
        if inc is not None:
            by = 1 if inc in ENG_SEMS else 16
            self.cnt[inc] += by
            isem, iby = sems[inc], by
        else:
            isem, iby = None, 0

        def run(e, fn=fn, wlist=wlist, isem=isem, iby=iby):
            for s, v in wlist:
                e.wait_ge(s, v)
            inst = fn(e)
            if isem is not None:
                inst.then_inc(isem, iby)

        self.ops[eng].append(run)
        return self.cnt[inc] if inc is not None else None

    def wait(self, eng, waits):
        sems = self.sems
        wlist = [(sems[s], v) for (s, v) in waits if v > 0]

        def run(e, wlist=wlist):
            for s, v in wlist:
                e.wait_ge(s, v)

        self.ops[eng].append(run)


def build_nc(upto=7, debug=False):
    nc = bass.Bass("TRN2", target_bir_lowering=False)
    dkind = "ExternalOutput" if debug else "Internal"

    xa = nc.dram_tensor("xa", [SEQ, D], F32, kind="ExternalInput").ap()
    xo = nc.dram_tensor("xo", [MROWS, D], F32, kind="ExternalInput").ap()
    wq = nc.dram_tensor("wq", [HH, 128, D], F32, kind="ExternalInput").ap()
    wk = nc.dram_tensor("wk", [HH, 128, D], F32, kind="ExternalInput").ap()
    wv = nc.dram_tensor("wv", [HH, 128, D], F32, kind="ExternalInput").ap()
    wo = nc.dram_tensor("wo", [8, 128, 32 * 512], F32, kind="ExternalInput").ap()
    gwh = nc.dram_tensor("gwh", [NIT, 128, D], F32, kind="ExternalInput").ap()
    uwh = nc.dram_tensor("uwh", [NIT, 128, D], F32, kind="ExternalInput").ap()
    dwh = nc.dram_tensor("dwh", [32, 128, INTER], F32, kind="ExternalInput").ap()
    bqh = nc.dram_tensor("bqh", [128, HH], F32, kind="ExternalInput").ap()
    bkh = nc.dram_tensor("bkh", [128, HH], F32, kind="ExternalInput").ap()
    bvh = nc.dram_tensor("bvh", [128, HH], F32, kind="ExternalInput").ap()
    bo = nc.dram_tensor("bo", [1, D], F32, kind="ExternalInput").ap()
    gsc = nc.dram_tensor("gsc", [1, 1], F32, kind="ExternalInput").ap()
    usc = nc.dram_tensor("usc", [1, 1], F32, kind="ExternalInput").ap()
    dsc = nc.dram_tensor("dsc", [1, 1], F32, kind="ExternalInput").ap()
    identw = nc.dram_tensor("identw", [128, 128], BF16, kind="ExternalInput").ap()
    identfw = nc.dram_tensor("identfw", [128, 128], F32, kind="ExternalInput").ap()
    maskw = nc.dram_tensor("maskw", [128, 128], F32, kind="ExternalInput").ap()
    gidx = nc.dram_tensor("gidx", [128, 32], I32, kind="ExternalInput").ap()
    outp = nc.dram_tensor("outp", [MROWS, D], F32, kind="ExternalOutput").ap()

    K_s = nc.dram_tensor("K_s", [HH, 128, SEQ], BF16, kind=dkind).ap()
    Q_s = nc.dram_tensor("Q_s", [HH, 128, SEQ], BF16, kind=dkind).ap()
    V_s = nc.dram_tensor("V_s", [HH, 128, SEQ], BF16, kind=dkind).ap()
    cc_in = nc.dram_tensor("cc_in", [OH, SEQ], BF16, kind=dkind).ap()
    cc_out = nc.dram_tensor("cc_out", [2 * OH, SEQ], BF16, kind=dkind).ap()
    xn_d = nc.dram_tensor("xn_d", [D, SEQ], BF16, kind=dkind).ap()
    dw_bf = nc.dram_tensor("dw_bf", [32, 128, INTER], BF16, kind="Internal").ap()
    inter_d = nc.dram_tensor("inter_d", [NIT, 128, MROWS], BF16, kind=dkind).ap()
    g_s = nc.dram_tensor("g_s", [MROWS, D], F32, kind=dkind).ap()
    led_d = nc.dram_tensor("led_d", [16, 256], F32, kind="Internal").ap()
    xnd = (nc.dram_tensor("xnd", [128, 65536], BF16, kind="ExternalOutput").ap()
           if debug and upto <= 1 else None)

    with ExitStack() as stack:
        sems = {n: stack.enter_context(nc.semaphore(n)) for n in SEM_NAMES}

        AB = stack.enter_context(nc.sbuf_tensor("AB", [128, 82000], BF16))
        AFR = stack.enter_context(nc.sbuf_tensor("AFR", [128, 9728], F32))
        ident = stack.enter_context(nc.sbuf_tensor("ident", [128, 128], BF16))
        identf = stack.enter_context(nc.sbuf_tensor("identf", [128, 128], F32))
        maskt = stack.enter_context(nc.sbuf_tensor("maskt", [128, 128], F32))
        biasq = stack.enter_context(nc.sbuf_tensor("biasq", [128, HH], F32))
        biask = stack.enter_context(nc.sbuf_tensor("biask", [128, HH], F32))
        biasv = stack.enter_context(nc.sbuf_tensor("biasv", [128, HH], F32))
        svec = stack.enter_context(nc.sbuf_tensor("svec", [128, 8], F32))
        stat = stack.enter_context(nc.sbuf_tensor("stat", [128, 24], F32))
        gidx_sb = stack.enter_context(nc.sbuf_tensor("gidx_sb", [128, 32], I32))
        rrmat = stack.enter_context(nc.sbuf_tensor("rrmat", [128, 32], F32))
        led = stack.enter_context(nc.sbuf_tensor("led", [16, 256], F32))
        rr_war = [0]

        PSF = [stack.enter_context(nc.psum_tensor(f"psf{i}", [128, 512], F32))
               for i in range(6)]
        PSB = [stack.enter_context(nc.psum_tensor(f"psb{i}", [128, 1024], BF16))
               for i in range(2)]

        p = Plan(sems)
        c = p.cnt

        psf_free = [("PE", 0)] * 6
        psb_free = [("PE", 0)] * 2

        def ab(off, n):
            return AB[:, off:off + n]

        def af(off, n):
            return AFR[:, off:off + n]

        for dst_t, src_t in ((ident, identw), (identf, identfw),
                             (maskt, maskw), (biasq, bqh), (biask, bkh),
                             (biasv, bvh), (gidx_sb, gidx)):
            p.emit("sp", lambda e, d=dst_t, s=src_t: e.dma_start(
                out=d[:, :], in_=s), inc="C0")
        for col, src_t in ((0, gsc), (1, usc), (2, dsc)):
            p.emit("sp", lambda e, col=col, s=src_t: e.dma_start(
                out=svec[:, col:col + 1],
                in_=s.partition_broadcast(128)[:, 0, :]), inc="C0")
        consts_ld = c["C0"]
        for eng in ("pe", "dve", "act"):
            p.wait(eng, [("C0", consts_ld)])
        usds_dv = p.emit(
            "dve", lambda e: e.tensor_mul(svec[:, 3:4], svec[:, 1:2],
                                          svec[:, 2:3]), inc="DV")

        def ln_pass(src_ap, nrow_tiles, xn_off, mwidth, xbuf_off, xnrm_off):
            dv0, pe0 = c["DV"], c["PE"]
            xd_done = [0, 0]
            dv_free = [("DV", dv0), ("DV", dv0)]
            nrm_free = [dv0, dv0]
            nrm_pe_free = [pe0, pe0]
            cks = []
            for t in range(nrow_tiles):
                sl = t % 2
                sb = 16 * sl
                xsem = f"X{sl}"
                xb = af(xbuf_off + sl * 4096, 4096)
                xpsem = f"XP{sl}"
                xd_done[sl] = p.emit(
                    "sp",
                    lambda e, src=src_ap, t=t, sl=sl: e.dma_start(
                        out=af(xbuf_off + sl * 4096, 2048),
                        in_=src[t * 128:(t + 1) * 128, 0:2048]),
                    waits=[dv_free[sl]], inc=xsem)
                xp_done = p.emit(
                    "pool",
                    lambda e, src=src_ap, t=t, sl=sl: e.dma_start(
                        out=af(xbuf_off + sl * 4096 + 2048, 2048),
                        in_=src[t * 128:(t + 1) * 128, 2048:4096]),
                    waits=[dv_free[sl]], inc=xpsem)
                xr = ab(xnrm_off + sl * 4096, 4096)
                p.emit("dve", lambda e, xb=xb, sb=sb: e.reduce_sum(
                    stat[:, sb + 0:sb + 1], xb, AX.X),
                    waits=[(xsem, xd_done[sl]), (xpsem, xp_done)])
                ac_sq = p.emit("act", lambda e, xb=xb, xr=xr: e.activation(
                    xr, xb, AF.Square),
                    waits=[(xsem, xd_done[sl]), (xpsem, xp_done),
                           ("DV", nrm_free[sl])], inc="AC")
                p.emit("dve", lambda e, xr=xr, sb=sb: e.reduce_sum(
                    stat[:, sb + 1:sb + 2], xr, AX.X), waits=[("AC", ac_sq)])
                p.emit("dve", lambda e, sb=sb: e.tensor_scalar(
                    stat[:, sb + 2:sb + 3], stat[:, sb + 0:sb + 1],
                    1.0 / D, None, ALU.mult))
                p.emit("dve", lambda e, sb=sb: e.tensor_mul(
                    stat[:, sb + 3:sb + 4], stat[:, sb + 2:sb + 3],
                    stat[:, sb + 2:sb + 3]))
                p.emit("dve", lambda e, sb=sb: e.tensor_scalar(
                    stat[:, sb + 4:sb + 5], stat[:, sb + 1:sb + 2],
                    1.0 / D, 1.0e-5, ALU.mult, ALU.add))
                dv_var = p.emit("dve", lambda e, sb=sb: e.tensor_sub(
                    stat[:, sb + 4:sb + 5], stat[:, sb + 4:sb + 5],
                    stat[:, sb + 3:sb + 4]), inc="DV")
                ac_std = p.emit("act", lambda e, sb=sb: e.activation(
                    stat[:, sb + 5:sb + 6], stat[:, sb + 4:sb + 5], AF.Sqrt),
                    waits=[("DV", dv_var)], inc="AC")
                p.emit("dve", lambda e, sb=sb: e.reciprocal(
                    stat[:, sb + 6:sb + 7], stat[:, sb + 5:sb + 6]),
                    waits=[("AC", ac_std)])
                p.emit("dve", lambda e, sb=sb: e.tensor_mul(
                    stat[:, sb + 7:sb + 8], stat[:, sb + 2:sb + 3],
                    stat[:, sb + 6:sb + 7]))
                dv_nmr = p.emit("dve", lambda e, sb=sb: e.tensor_scalar(
                    stat[:, sb + 7:sb + 8], stat[:, sb + 7:sb + 8],
                    -1.0, None, ALU.mult), inc="DV")
                ac_xn = p.emit("act", lambda e, xb=xb, xr=xr, sb=sb:
                               e.activation(
                                   xr, xb, AF.Identity,
                                   bias=stat[:, sb + 7:sb + 8],
                                   scale=stat[:, sb + 6:sb + 7]),
                               waits=[("DV", dv_nmr),
                                      ("PE", nrm_pe_free[sl])], inc="AC")
                dv_free[sl] = ("AC", ac_xn)
                for g in range(4):
                    bank = g % 2
                    pe_tr = None
                    for j in range(8):
                        kt = g * 8 + j
                        pe_tr = p.emit(
                            "pe",
                            lambda e, xr=xr, kt=kt, j=j, bank=bank: e.transpose(
                                PSB[bank][:, j * 128:(j + 1) * 128],
                                xr[:, kt * 128:(kt + 1) * 128], ident[:, :]),
                            waits=([("AC", ac_xn), psb_free[bank]]
                                   if j == 0 else []),
                            inc=("PE" if j == 7 else None))
                    base = xn_off + (g * 8) * mwidth + t * 128

                    def evac(e, bank=bank, base=base, mw=mwidth):
                        dst = AB[:, base:base + 8 * mw]
                        v = dst.rearrange("p (k m) -> p k m", k=8)[:, :, 0:128]
                        src = PSB[bank][:, :].rearrange("p (k m) -> p k m", k=8)
                        return e.tensor_copy(v, src)
                    dv_ev = p.emit("dve", evac, waits=[("PE", pe_tr)], inc="DV")
                    psb_free[bank] = ("DV", dv_ev)
                nrm_free[sl] = c["DV"]
                nrm_pe_free[sl] = c["PE"]
                if t % 4 == 3:
                    cks.append(c["DV"])
            return cks

        def finish():
            block = stack.enter_context(nc.Block())
            block.tensor(lambda e: [f(e) for f in p.ops["pe"]])
            block.vector(lambda e: [f(e) for f in p.ops["dve"]])
            block.scalar(lambda e: [f(e) for f in p.ops["act"]])
            block.gpsimd(lambda e: [f(e) for f in p.ops["pool"]])
            block.sync(lambda e: [f(e) for f in p.ops["sp"]])
            return nc

        XN_OFF = 0
        XNRM_OFF = 65536
        WBUF_OFF = 73728
        KQSTAGE_OFF = 65536
        VFM_OFF = 66560
        VRM_OFF = 67584
        wd_done = [0, 0]
        p.emit("pool", lambda e: e.dma_start(
            out=ab(WBUF_OFF, 4096), in_=wk[0]), inc="W0")
        wd_done[0] = c["W0"]
        p.emit("pool", lambda e: e.dma_start(
            out=ab(WBUF_OFF + 4096, 4096), in_=wv[0]), inc="W1")
        wd_done[1] = c["W1"]
        p1_cks = ln_pass(xa, NQT, XN_OFF, SEQ, 0, XNRM_OFF)
        p1_dv = p1_cks[-1]
        if upto <= 1:
            if debug:
                xn_st = p.emit("sp", lambda e: e.dma_start(
                    out=xnd, in_=AB[:, 0:65536]), waits=[("DV", p1_dv)],
                    inc="GA")
                p.wait("sp", [("GA", xn_st)])
            return finish()

        wfree = [0, 0]
        kq_free = [0, 0]
        vfm_free = [0, 0]
        vrm_free = [0, 0]
        mats = [(wk, biask, K_s), (wv, biasv, V_s), (wq, biasq, Q_s)]
        for ot in range(HH):
            if 8 <= ot < 16:
                for j2 in range(4):
                    kt = (ot - 8) * 4 + j2
                    ssl = kt % 2
                    p.emit("sp", lambda e, kt=kt: e.dma_start(
                        out=xn_d[kt * 128:(kt + 1) * 128, :],
                        in_=ab(XN_OFF + kt * SEQ, SEQ)),
                        waits=([("DV", p1_dv)] if kt == 0 else []),
                        inc=f"XD{ssl}")
            if 2 <= ot:
                cot = ot - 2
                p.emit("pool", lambda e, cot=cot: e.dma_start(
                    out=dw_bf[cot], in_=dwh[cot]), inc="DWC0")
            for mi, (wmat, bvec, dst) in enumerate(mats):
                it = ot * 3 + mi
                sl = it % 2
                wsem = f"W{sl}"
                wb = ab(WBUF_OFF + sl * 4096, 4096)
                if it >= 2:
                    wd_done[sl] = p.emit(
                        "pool", lambda e, wmat=wmat, ot=ot, wb=wb:
                        e.dma_start(out=wb, in_=wmat[ot]),
                        waits=[("PE", wfree[sl])], inc=wsem)
                vsl = ot % 2
                vr = ab(VRM_OFF + vsl * 2048, 2048)
                v_dv = [None] * 4

                def emit_vtrans(mc):
                    ssl = mc % 2
                    vf = ab(VFM_OFF + ssl * 512, 512)
                    bank = mc % 2
                    pe_tr = None
                    for ms in range(4):
                        pe_tr = p.emit(
                            "pe",
                            lambda e, vf=vf, ms=ms, bank=bank: e.transpose(
                                PSB[bank][:, ms * 128:(ms + 1) * 128],
                                vf[:, ms * 128:(ms + 1) * 128],
                                ident[:, :]),
                            waits=([v_dv[mc], psb_free[bank]]
                                   if ms == 0 else []),
                            inc=("PE" if ms == 3 else None))
                    vfm_free[ssl] = pe_tr
                    dv_ev = p.emit(
                        "dve",
                        lambda e, vr=vr, mc=mc, bank=bank: e.tensor_copy(
                            vr[:, mc * 512:(mc + 1) * 512],
                            PSB[bank][:, 0:512]),
                        waits=([("PE", pe_tr), (f"SV{vsl}", vrm_free[vsl])]
                               if mc == 0 else [("PE", pe_tr)]),
                        inc="DV")
                    psb_free[bank] = ("DV", dv_ev)

                for mc in range(4):
                    for kt in range(NKT):
                        wl = []
                        if kt == 0:
                            wl.append(psf_free[mc])
                            if mc == 0:
                                wl.append((wsem, wd_done[sl]))
                            if it == 0:
                                wl.append(("DV", p1_cks[mc]))
                        p.emit(
                            "pe",
                            lambda e, wb=wb, kt=kt, mc=mc: e.matmul(
                                PSF[mc][:, :],
                                wb[:, kt * 128:(kt + 1) * 128],
                                ab(XN_OFF + kt * SEQ + mc * 512, 512),
                                start=(kt == 0), stop=(kt == NKT - 1),
                                skip_group_check=True),
                            waits=wl,
                            inc=("PE" if kt == NKT - 1 else None))
                    pe_mc = c["PE"]
                    if mi != 1:
                        ssl = mc % 2
                        st_ = ab(KQSTAGE_OFF + ssl * 512, 512)
                        dv_e = p.emit(
                            "dve",
                            lambda e, mc=mc, st_=st_, bvec=bvec, ot=ot:
                            e.tensor_scalar(st_, PSF[mc][:, :],
                                            bvec[:, ot:ot + 1], None,
                                            ALU.add),
                            waits=[("PE", pe_mc),
                                   (f"SK{ssl}", kq_free[ssl])],
                            inc="DV")
                        psf_free[mc] = ("DV", dv_e)
                        kq_free[ssl] = p.emit(
                            "sp",
                            lambda e, dst=dst, ot=ot, mc=mc, st_=st_:
                            e.dma_start(
                                out=dst[ot][:, mc * 512:(mc + 1) * 512],
                                in_=st_),
                            waits=[("DV", dv_e)], inc=f"SK{ssl}")
                    else:
                        ssl = mc % 2
                        vf = ab(VFM_OFF + ssl * 512, 512)
                        dv_e = p.emit(
                            "dve",
                            lambda e, mc=mc, vf=vf, ot=ot: e.tensor_scalar(
                                vf, PSF[mc][:, :], biasv[:, ot:ot + 1],
                                None, ALU.add),
                            waits=[("PE", pe_mc), ("PE", vfm_free[ssl])],
                            inc="DV")
                        psf_free[mc] = ("DV", dv_e)
                        v_dv[mc] = ("DV", dv_e)
                        if mc > 0:
                            emit_vtrans(mc - 1)
                if mi == 1:
                    emit_vtrans(3)
                    vrm_free[vsl] = p.emit(
                        "sp",
                        lambda e, ot=ot, vr=vr: e.dma_start(
                            out=V_s[ot], in_=vr),
                        waits=[("DV", c["DV"])], inc=f"SV{vsl}")
                wfree[sl] = c["PE"]
        p2_stores = [("SK0", c["SK0"]), ("SK1", c["SK1"]),
                     ("SV0", c["SV0"]), ("SV1", c["SV1"])]
        p2_pe = c["PE"]
        p2sk = (c["SK0"], c["SK1"])
        xd_all = [("XD0", c["XD0"]), ("XD1", c["XD1"])]
        if upto <= 2:
            p.wait("sp", p2_stores + xd_all)
            return finish()

        XNF_OFF = 0
        GWB_OFF = 32768
        UWB_OFF = 36864
        ISTG_OFF = 40960
        KB_OFF = 43008
        VB_OFF = 47104
        QB_OFF = 51200
        PR_OFF = 55296
        PT_OFF = 59392
        AT_OFF = 79872
        RB_OFF = 8192
        xn_g = xn_d.rearrange("d (h m) -> (d h) m", h=2)
        for t in range(32):
            wl = ([("PE", p2_pe)] + xd_all) if t == 0 else []
            p.emit(
                "pool", lambda e, t=t: e.indirect_dma_start(
                    out=ab(XNF_OFF + t * 1024, 1024),
                    out_offset=None,
                    in_=xn_g,
                    in_offset=bass.IndirectOffsetOnAxis(
                        gidx_sb[:, t:t + 1], 0)),
                waits=wl, inc="XF0")
        xf_all = c["XF0"]

        gwc = {}
        uwc = {}
        gu = {"next": 0, "gw_free": p2_pe, "uw_free": p2_pe,
              "g0_pe": 0, "sil_ac": 0}
        ist_free = [0, 0]
        first_mul = [True]

        def emit_gw_load(it):
            sl = it % 2
            wl = ([("PE", p2_pe)] + xd_all if it == 0
                  else [("PE", gu["gw_free"])])
            p.emit("pool", lambda e, it=it: e.dma_start(
                out=ab(GWB_OFF, 4096), in_=gwh[it]), waits=wl, inc=f"W{sl}")
            gwc[it] = c[f"W{sl}"]

        def emit_uw_load(it):
            sl = it % 2
            wl = ([("PE", p2_pe)] + xd_all if it == 0
                  else [("PE", gu["uw_free"])])
            p.emit("pool", lambda e, it=it: e.dma_start(
                out=ab(UWB_OFF, 4096), in_=uwh[it]), waits=wl, inc=f"UW{sl}")
            uwc[it] = c[f"UW{sl}"]

        def emit_gu_group(gi):
            it, j = gi // 4, gi % 4
            sl = it % 2
            bk = 2 + (j % 2)
            woff = GWB_OFF if j < 2 else UWB_OFF
            mc = j % 2
            wl0 = [psf_free[bk]]
            if j == 0:
                wl0.append((f"W{sl}", gwc[it]))
                if it == 0:
                    wl0.append(("XF0", xf_all))
            elif j == 2:
                wl0.append((f"UW{sl}", uwc[it]))
            for kt in range(NKT):
                p.emit(
                    "pe", lambda e, kt=kt, mc=mc, bk=bk, woff=woff: e.matmul(
                        PSF[bk][:, :],
                        AB[:, woff + kt * 128:woff + (kt + 1) * 128],
                        ab(XNF_OFF + kt * MROWS + mc * 512, 512),
                        start=(kt == 0), stop=(kt == NKT - 1),
                        skip_group_check=True),
                    waits=(wl0 if kt == 0 else []),
                    inc=("PE" if kt == NKT - 1 else None))
            gpe = c["PE"]
            gu["end_pe"] = gpe
            if j == 0:
                gu["g0_pe"] = gpe
            elif j == 1:
                gu["gw_free"] = gpe
                if it + 1 < NIT:
                    emit_gw_load(it + 1)
                for mc2, bkk in ((0, 2), (1, 3)):
                    dv1 = p.emit(
                        "dve", lambda e, mc2=mc2, bkk=bkk: e.tensor_scalar(
                            af(4096 + mc2 * 512, 512), PSF[bkk][:, :],
                            svec[:, 0:1], None, ALU.mult),
                        waits=[("PE", gpe if mc2 else gu["g0_pe"]),
                               ("DV", usds_dv), ("AC", gu["sil_ac"])],
                        inc="DV")
                    psf_free[bkk] = ("DV", dv1)
                gu["sil_ac"] = p.emit(
                    "act", lambda e: e.activation(
                        af(5120, 1024), af(4096, 1024), AF.Silu),
                    waits=[("DV", c["DV"])], inc="AC")
            else:
                mc2 = j - 2
                wlm = [("AC", gu["sil_ac"]), ("PE", gpe),
                       (f"SI{sl}", ist_free[sl])]
                if first_mul[0]:
                    wlm += xd_all
                    first_mul[0] = False
                dv2 = p.emit(
                    "dve", lambda e, it=it, mc2=mc2, bk=bk: e.tensor_mul(
                        ab(ISTG_OFF + (it % 2) * 1024 + mc2 * 512, 512),
                        af(5120 + mc2 * 512, 512), PSF[bk][:, :]),
                    waits=wlm, inc="DV")
                psf_free[bk] = ("DV", dv2)
                if j == 3:
                    gu["uw_free"] = gpe
                    if it + 1 < NIT:
                        emit_uw_load(it + 1)
                    dvm = c["DV"]
                    ist_free[sl] = p.emit(
                        "pool", lambda e, it=it: e.dma_start(
                            out=inter_d[it],
                            in_=ab(ISTG_OFF + (it % 2) * 1024, 1024)),
                        waits=[("DV", dvm)], inc=f"SI{sl}")
                    if it % 4 == 2 and it < 72:
                        cot = 14 + (it - 2) // 4
                        p.emit("pool", lambda e, cot=cot: e.dma_start(
                            out=dw_bf[cot], in_=dwh[cot]), inc="DWC0")

        emit_gw_load(0)
        emit_uw_load(0)

        def pt_col(t, q):
            g = t // 4
            return PT_OFF + PT_GOFF[g] + (t % 4) * GW_G[g] + (q - 4 * g) * 128

        kv_free = [p2_pe, p2_pe]
        at_free = [0, 0]
        pr_free = [0, 0]
        sc_free = [("AC", 0), ("AC", 0)]
        rb_free = [0, 0]
        led_st = [0, 0]
        rr_tr = [0, 0]
        prev_pt = ("DV", 0)
        prev_vb = None
        led_dv = [0, 0]
        first_pt = [True]
        first_exp = [True]
        p.wait("sp", p2_stores)

        def emit_pv(h, vb, pt_mark):
            for mc in range(4):
                bk = 4 + mc % 2
                ntk = 4 * mc + 4
                rsl = mc % 2
                rb = af(RB_OFF + rsl * 512, 512)
                rb3 = rb.rearrange("p (j c) -> p j c", j=4)
                rb_ld = p.emit(
                    "sp", lambda e, h=h, mc=mc, rb3=rb3: e.dma_start(
                        out=rb3,
                        in_=led_d[4 * mc:4 * mc + 4,
                                  (h % 2) * 128:(h % 2 + 1) * 128]
                        .partition_broadcast(128)),
                    waits=[(f"LS{h % 2}", led_st[h % 2]), ("DV", rb_free[rsl])],
                    inc=f"RB{rsl}")
                pe_pv = None
                for t in range(ntk):
                    off = max(0, t * 128 - mc * 512)
                    nn = 512 - off
                    wl = []
                    if t == 0:
                        wl.append(pt_mark)
                        wl.append(psf_free[bk])
                    pe_pv = p.emit(
                        "pe", lambda e, vb=vb, t=t, mc=mc, off=off, nn=nn,
                        bk=bk: e.matmul(
                            PSF[bk][:, off:off + nn],
                            vb[:, t * 128:(t + 1) * 128],
                            ab(pt_col(t, 4 * mc) + off, nn),
                            start=(t == 0), stop=(t == ntk - 1),
                            skip_group_check=True),
                        waits=wl, inc=("PE" if t == ntk - 1 else None))
                asl = mc % 2
                at = ab(AT_OFF + asl * 512, 512)
                dv_a = p.emit(
                    "dve", lambda e, at=at, bk=bk, rb=rb: e.tensor_mul(
                        at, PSF[bk][:, :], rb),
                    waits=[("PE", pe_pv), (f"RB{rsl}", rb_ld),
                           (f"SA{asl}", at_free[asl])],
                    inc="DV")
                psf_free[bk] = ("DV", dv_a)
                rb_free[rsl] = dv_a
                at_free[asl] = p.emit(
                    "sp", lambda e, h=h, mc=mc, at=at: e.dma_start(
                        out=cc_in[h * 128:(h + 1) * 128,
                                  mc * 512:(mc + 1) * 512], in_=at),
                    waits=[("DV", dv_a)], inc=f"SA{asl}")
            return c["PE"]

        ac_e_mark = {}
        bias_mark = {}
        for h in range(HH):
            sl = h % 2
            ksem = f"K{sl}"
            kb = ab(KB_OFF + sl * 2048, 2048)
            vb = ab(VB_OFF + sl * 2048, 2048)
            qb = ab(QB_OFF + sl * 2048, 2048)
            wl0 = [("PE", kv_free[sl])]
            if h == 0:
                wl0 += p2_stores + xd_all
            p.emit("sp", lambda e, h=h, kb=kb: e.dma_start(
                out=kb, in_=K_s[h]), waits=wl0, inc=ksem)
            p.emit("sp", lambda e, h=h, vb=vb: e.dma_start(
                out=vb, in_=V_s[h]), inc=ksem)
            kd_done = p.emit("sp", lambda e, h=h, qb=qb: e.dma_start(
                out=qb, in_=Q_s[h]), inc=ksem)

            for i in range(NQT + 2):
                if i < NQT:
                    qt = i
                    Kt = 128 * (qt + 1)
                    nkc = (Kt + 511) // 512
                    psl = qt % 2
                    sc = af(psl * 2048, 2048)
                    sc_ac = []
                    for kc in range(nkc):
                        nn = min(512, Kt - kc * 512)
                        bk = kc % 2
                        wl = [psf_free[bk]]
                        if qt == 0 and kc == 0:
                            wl.append((ksem, kd_done))
                        pe_s = p.emit(
                            "pe", lambda e, qb=qb, kb=kb, qt=qt, kc=kc,
                            nn=nn, bk=bk: e.matmul(
                                PSF[bk][:, 0:nn],
                                qb[:, qt * 128:(qt + 1) * 128],
                                kb[:, kc * 512:kc * 512 + nn],
                                start=True, stop=True,
                                skip_group_check=True),
                            waits=wl, inc="PE")
                        wl2 = [("PE", pe_s)]
                        if kc <= 1:
                            wl2.append(sc_free[psl])
                        if kc % 2 == 0:
                            ev = p.emit(
                                "act", lambda e, sc=sc, kc=kc, nn=nn, bk=bk:
                                e.copy(sc[:, kc * 512:kc * 512 + nn],
                                       PSF[bk][:, 0:nn]),
                                waits=wl2, inc="AC")
                            psf_free[bk] = ("AC", ev)
                            sc_ac.append(("AC", ev))
                        else:
                            ev = p.emit(
                                "dve", lambda e, sc=sc, kc=kc, nn=nn, bk=bk:
                                e.tensor_copy(sc[:, kc * 512:kc * 512 + nn],
                                              PSF[bk][:, 0:nn]),
                                waits=wl2, inc="DV")
                            psf_free[bk] = ("DV", ev)
                            sc_ac.append(("DV", ev))
                if i - 2 >= 0:
                    pqt = i - 2
                    pac = ac_e_mark[(h, pqt)]
                    ntile = pqt + 1
                    for g0 in range(0, ntile, 4):
                        g = g0 // 4
                        gsz = min(4, ntile - g0)
                        bank = g % 2
                        pe_tr = None
                        for j in range(gsz):
                            t = g0 + j
                            pe_tr = p.emit(
                                "pe", lambda e, pr2=ab(
                                    PR_OFF + (pqt % 2) * 2048, 2048), t=t,
                                j=j, bank=bank: e.transpose(
                                    PSB[bank][:, j * 128:(j + 1) * 128],
                                    pr2[:, t * 128:(t + 1) * 128],
                                    ident[:, :]),
                                waits=([("AC", pac), psb_free[bank]]
                                       if j == 0 else []),
                                inc=("PE" if j == gsz - 1 else None))
                        base = pt_col(4 * g, pqt)
                        gww = GW_G[g]

                        def evacp(e, base=base, gsz=gsz, bank=bank, gww=gww):
                            dst = AB[:, base:base + gsz * gww]
                            v = dst.rearrange("p (k m) -> p k m",
                                              k=gsz)[:, :, 0:128]
                            src = PSB[bank][:, 0:gsz * 128].rearrange(
                                "p (k m) -> p k m", k=gsz)
                            return e.tensor_copy(v, src)
                        wle = [("PE", pe_tr)]
                        if first_pt[0]:
                            wle += xd_all
                            first_pt[0] = False
                        dv_pt = p.emit("dve", evacp, waits=wle, inc="DV")
                        psb_free[bank] = ("DV", dv_pt)
                    pr_free[pqt % 2] = c["PE"]
                if h > 0 and i == 0:
                    kv_free[1 - sl] = emit_pv(h - 1, prev_vb, prev_pt)
                if i < NQT:
                    qt = i
                    Kt = 128 * (qt + 1)
                    psl = qt % 2
                    sc = af(psl * 2048, 2048)
                    dv_m = p.emit(
                        "dve", lambda e, sc=sc, Kt=Kt: e.tensor_add(
                            sc[:, Kt - 128:Kt], sc[:, Kt - 128:Kt],
                            maskt[:, :]),
                        waits=sc_ac, inc="DV")
                    p.emit("dve", lambda e, sc=sc, Kt=Kt, psl=psl:
                           e.reduce_max(stat[:, 8 + psl:9 + psl],
                                        sc[:, 0:Kt], AX.X, negate=True),
                           waits=[("DV", dv_m)])
                    bias_mark[(h, qt)] = p.emit(
                        "dve", lambda e, psl=psl: e.tensor_scalar(
                            stat[:, 10 + psl:11 + psl],
                            stat[:, 8 + psl:9 + psl], SCALE, None, ALU.mult),
                        waits=[("AC", ac_e_mark.get((h, qt - 2), 0))],
                        inc="DV")
                if i - 1 >= 0 and i - 1 < NQT:
                    qt = i - 1
                    Kt = 128 * (qt + 1)
                    psl = qt % 2
                    sc = af(psl * 2048, 2048)
                    pr = ab(PR_OFF + psl * 2048, 2048)
                    wle = [("DV", bias_mark[(h, qt)]),
                           ("PE", pr_free[psl]),
                           ("DV", rr_war[0])]
                    if first_exp[0]:
                        wle += xd_all
                        first_exp[0] = False
                    ac_e = p.emit(
                        "act", lambda e, sc=sc, pr=pr, Kt=Kt, psl=psl:
                        e.activation(
                            pr[:, 0:Kt], sc[:, 0:Kt], AF.Exp,
                            bias=stat[:, 10 + psl:11 + psl], scale=SCALE,
                            accum_out=stat[:, 12 + psl:13 + psl]),
                        waits=wle, inc="AC")
                    ac_e_mark[(h, qt)] = ac_e
                    sc_free[psl] = ("AC", ac_e)
                    rr_war[0] = p.emit(
                        "dve", lambda e, h=h, qt=qt, psl=psl: e.reciprocal(
                            rrmat[:, (h % 2) * 16 + qt:
                                  (h % 2) * 16 + qt + 1],
                            stat[:, 12 + psl:13 + psl]),
                        waits=[("AC", ac_e), ("PE", rr_tr[h % 2])],
                        inc="DV")
                if i < NQT:
                    s_glob = h * NQT + i
                    tgt = (0 if s_glob < GU_S0 else
                           min(NGU, (s_glob - GU_S0 + 1) * NGU
                               // (HH * NQT - GU_S0)))
                    while gu["next"] < tgt:
                        emit_gu_group(gu["next"])
                        gu["next"] += 1
            pe_rr = p.emit(
                "pe", lambda e, h=h: e.transpose(
                    PSF[3][0:16, 0:128],
                    rrmat[:, (h % 2) * 16:(h % 2) * 16 + 16],
                    identf[:, :]),
                waits=[("DV", rr_war[0]), psf_free[3]], inc="PE")
            rr_tr[h % 2] = pe_rr
            led_dv[h % 2] = p.emit(
                "dve", lambda e, h=h: e.tensor_copy(
                    led[0:16, (h % 2) * 128:(h % 2 + 1) * 128],
                    PSF[3][0:16, 0:128]),
                waits=[("PE", pe_rr), (f"LS{h % 2}", led_st[h % 2])],
                inc="DV")
            psf_free[3] = ("DV", led_dv[h % 2])
            led_st[h % 2] = p.emit(
                "sp", lambda e, h=h: e.dma_start(
                    out=led_d[:, (h % 2) * 128:(h % 2 + 1) * 128],
                    in_=led[0:16, (h % 2) * 128:(h % 2 + 1) * 128]),
                waits=[("DV", led_dv[h % 2])], inc=f"LS{h % 2}")
            prev_pt = ("DV", c["DV"])
            prev_vb = vb
        kv_free[(HH - 1) % 2] = emit_pv(HH - 1, prev_vb, prev_pt)
        while gu["next"] < NGU:
            emit_gu_group(gu["next"])
            gu["next"] += 1
        att_stores = [("SA0", c["SA0"]), ("SA1", c["SA1"])]
        si_all = [("SI0", c["SI0"]), ("SI1", c["SI1"])]
        p4_pe = c["PE"]
        cc_g = cc_out.rearrange("d (b m) -> (d b) m", b=2)
        if upto <= 4:
            p.wait("sp", att_stores)
            p.wait("pool", si_all + [("XF0", xf_all)])
            return finish()

        DW2_OFF = 44032

        def int_col(mch, il):
            if il < 30:
                return (66560 if mch == 0 else 28672) + il * 512
            return (il - 30) * 512

        def emit_int_loads(mch, il0, il1, wl_first):
            first = [True, True]
            for it in range(il0, il1):
                q = it % 2
                isl = q + 2 * mch
                wl = wl_first if first[q] else []
                first[q] = False
                p.emit("pool" if q == 0 else "sp",
                       lambda e, it=it, mch=mch: e.dma_start(
                           out=ab(int_col(mch, it), 512),
                           in_=inter_d[it][:, mch * 512:(mch + 1) * 512]),
                       waits=wl, inc=f"IN{isl}")

        DWSEM = ("W0", "DWS")
        dwc_tot = c["DWC0"]

        def emit_dw_load(ot, sl2, first=False):
            wl = [("PE", dw_free[sl2])]
            if first:
                wl.append(("DWC0", dwc_tot))
            p.emit("pool" if sl2 == 0 else "sp",
                   lambda e, ot=ot, sl2=sl2: e.dma_start(
                       out=AB[:, DW2_OFF + sl2 * INTER:
                              DW2_OFF + sl2 * INTER + INTER],
                       in_=dw_bf[ot]),
                   waits=wl, inc=DWSEM[sl2])
            dwc[sl2] = c[DWSEM[sl2]]

        dw_free = [p4_pe, p4_pe]
        dwc = [0, 0]
        fs_free = [("PE", 0), ("PE", 0)]
        gst_free = [0, 0]
        chunk_end = p4_pe
        cc_done = None
        ao_done1 = 0
        g_idx = 0
        for mch in range(2):
            if mch == 0:
                emit_int_loads(0, 30, NIT,
                               si_all + [("PE", gu["end_pe"])])
                emit_int_loads(0, 0, 30, [("PE", p4_pe)] + att_stores)
                in_tot = [("IN0", c["IN0"]), ("IN1", c["IN1"])]
                emit_dw_load(0, 0, first=True)
                emit_dw_load(1, 1, first=True)
                emit_int_loads(1, 0, 30, si_all + [("PE", p4_pe)])
                cc_done = p.emit(
                    "pool", lambda e: e.collective_compute(
                        "AllGather", ALU.bypass,
                        [[0, 1], [2, 3], [4, 5], [6, 7]],
                        [cc_in.opt()], [cc_out.opt()]),
                    waits=att_stores, inc="CC")
            else:
                emit_dw_load(0, 0)
                emit_int_loads(1, 30, NIT, [("PE", chunk_end)])
                emit_dw_load(1, 1)
                in_tot = [("IN2", c["IN2"]), ("IN3", c["IN3"])]
            for ot in range(32):
                sl2 = ot % 2
                if ot >= 2:
                    emit_dw_load(ot, sl2)
                if mch == 1 and ot == 8:
                    for t in range(15):
                        wl = ([("CC", cc_done), ("PE", chunk_end)]
                              if t == 0 else [])
                        p.emit(
                            "pool", lambda e, t=t: e.indirect_dma_start(
                                out=ab(66560 + t * 1024, 1024),
                                out_offset=None,
                                in_=cc_g,
                                in_offset=bass.IndirectOffsetOnAxis(
                                    gidx_sb[:, t:t + 1], 0)),
                            waits=wl, inc="GA")
                    ao_done1 = c["GA"]
                bk = (2 * g_idx) % 6
                bk2 = (2 * g_idx + 1) % 6
                g_idx += 1
                dwb = DW2_OFF + sl2 * INTER
                for il in range(NIT):
                    wl2 = []
                    if il == 0:
                        wl2 = [psf_free[bk], (DWSEM[sl2], dwc[sl2])]
                        if ot == 0:
                            wl2 += in_tot
                    p.emit(
                        "pe", lambda e, il=il, dwb=dwb, mch=mch, bk=bk:
                        e.matmul(
                            PSF[bk][:, :],
                            AB[:, dwb + il * 128:dwb + (il + 1) * 128],
                            ab(int_col(mch, il), 512),
                            start=(il == 0), stop=(il == NIT - 1),
                            skip_group_check=True),
                        waits=wl2, inc=("PE" if il == NIT - 1 else None))
                pe_d = c["PE"]
                dw_free[sl2] = pe_d
                fs = af(sl2 * 512, 512)
                dv_fs = p.emit(
                    "dve", lambda e, fs=fs, bk=bk: e.tensor_scalar(
                        fs, PSF[bk][:, :], svec[:, 3:4], None, ALU.mult),
                    waits=[("PE", pe_d), ("DV", usds_dv), fs_free[sl2]],
                    inc="DV")
                psf_free[bk] = ("DV", dv_fs)
                pe_tr = None
                for j in range(4):
                    pe_tr = p.emit(
                        "pe", lambda e, fs=fs, j=j, bk2=bk2: e.transpose(
                            PSF[bk2][:, j * 128:(j + 1) * 128],
                            fs[:, j * 128:(j + 1) * 128], identf[:, :]),
                        waits=([("DV", dv_fs), psf_free[bk2]]
                               if j == 0 else []),
                        inc=("PE" if j == 3 else None))
                fs_free[sl2] = ("PE", pe_tr)
                gst = af(1024 + sl2 * 512, 512)
                ac_g = p.emit(
                    "act", lambda e, gst=gst, bk2=bk2: e.activation(
                        gst, PSF[bk2][:, :], AF.Silu),
                    waits=[("PE", pe_tr), (f"SG{sl2}", gst_free[sl2])],
                    inc="AC")
                psf_free[bk2] = ("AC", ac_g)
                gst_free[sl2] = p.emit(
                    "sp", lambda e, mch=mch, ot=ot, gst=gst: e.dma_start(
                        out=g_s.rearrange("(ms pp) d -> pp ms d", pp=128)
                        [:, 4 * mch:4 * mch + 4, ot * 128:(ot + 1) * 128],
                        in_=gst.rearrange("pp (ms oc) -> pp ms oc", ms=4)),
                    waits=[("AC", ac_g)], inc=f"SG{sl2}")
            chunk_end = c["PE"]
        down_end = c["PE"]
        g_stores = [("SG0", c["SG0"]), ("SG1", c["SG1"])]
        if upto <= 5:
            p.wait("sp", g_stores)
            p.wait("pool", [("CC", c["CC"]), ("GA", c["GA"])])
            return finish()

        WO_OFF = 17408

        def ao_col(dt):
            return 66560 + dt * 1024 if dt < 15 else (dt - 15) * 1024

        ao_all = None
        bo_ld = None
        NCHUNK = 12
        wo_done = [0] * NCHUNK
        wo_free = [down_end] * NCHUNK
        XC_OFF = 4096
        GC_OFF = 5120
        S1_OFF = 6144
        OT_OFF = 7168
        xg_free = [0, 0]
        s1_free = [0, 0]
        ot_free = [0, 0]
        pending_out = [None]
        chunk_id = 0
        for oc in range(8):
            for dg in range(8):
                ci = chunk_id % NCHUNK
                wo_done[ci] = p.emit(
                    "pool", lambda e, oc=oc, dg=dg, ci=ci: e.dma_start(
                        out=ab(WO_OFF + ci * 2048, 2048),
                        in_=wo[oc][:, dg * 2048:(dg + 1) * 2048]),
                    waits=[("PE", wo_free[ci])],
                    inc=f"WO{ci}")
                chunk_id += 1
                if oc == 0 and dg == 2:
                    for t in range(15, 32):
                        wl = ([("CC", cc_done), ("PE", down_end),
                               ("GA", ao_done1)] if t == 15 else [])
                        p.emit(
                            "pool", lambda e, t=t: e.indirect_dma_start(
                                out=ab(ao_col(t), 1024),
                                out_offset=None,
                                in_=cc_g,
                                in_offset=bass.IndirectOffsetOnAxis(
                                    gidx_sb[:, t:t + 1], 0)),
                            waits=wl, inc="GA")
                    ao_all = c["GA"]
            if oc == 0:
                bo_ld = p.emit("sp", lambda e: e.dma_start(
                    out=af(0, D), in_=bo.partition_broadcast(128)[:, 0, :]),
                    waits=[("GA", ao_all)] + g_stores, inc="B0")
            base_chunk = chunk_id - 8
            for gpi, mts in enumerate(((0, 1, 2), (3, 4, 5), (6, 7))):
                banks = {mt: (gpi * 3 + i) % 6 for i, mt in enumerate(mts)}
                for dt in range(32):
                    ci = (base_chunk + dt // 4) % NCHUNK
                    dl = dt % 4
                    for mt in mts:
                        bk = banks[mt]
                        wl = []
                        if dt == 0:
                            wl.append(psf_free[bk])
                            if mt == mts[0]:
                                wl.append(("GA", ao_done1))
                        if dt == 15 and mt == mts[0]:
                            wl.append(("GA", ao_all))
                        if dl == 0 and mt == mts[0]:
                            wl.append((f"WO{ci}", wo_done[ci]))
                        p.emit(
                            "pe", lambda e, dt=dt, mt=mt, ci=ci, dl=dl,
                            bk=bk: e.matmul(
                                PSF[bk][:, :],
                                ab(ao_col(dt) + mt * 128, 128),
                                ab(WO_OFF + ci * 2048 + dl * 512, 512),
                                start=(dt == 0), stop=(dt == 31),
                                skip_group_check=True),
                            waits=wl,
                            inc=("PE" if (dt == 31 and mt == mts[-1])
                                 else None))
                pe_done = c["PE"]
                if gpi == 2:
                    for k in range(8):
                        wo_free[(base_chunk + k) % NCHUNK] = pe_done
                for mt in mts:
                    bk = banks[mt]
                    sl2 = mt % 2
                    xgsem = f"XG{sl2}"
                    xc = af(XC_OFF + sl2 * 512, 512)
                    gc = af(GC_OFF + sl2 * 512, 512)
                    s1 = af(S1_OFF + sl2 * 512, 512)
                    ot_ = af(OT_OFF + sl2 * 512, 512)
                    p.emit(
                        "sp", lambda e, mt=mt, oc=oc, xc=xc: e.dma_start(
                            out=xc, in_=xo[mt * 128:(mt + 1) * 128,
                                           oc * 512:(oc + 1) * 512]),
                        waits=([("DV", xg_free[sl2])] +
                               ([("GA", ao_all), ("PE", down_end)]
                                if oc == 0 else [])),
                        inc=xgsem)
                    xg_ld = p.emit(
                        "sp", lambda e, mt=mt, oc=oc, gc=gc: e.dma_start(
                            out=gc, in_=g_s[mt * 128:(mt + 1) * 128,
                                            oc * 512:(oc + 1) * 512]),
                        waits=(g_stores if oc == 0 else []), inc=xgsem)
                    if pending_out[0] is not None:
                        pending_out[0]()
                        pending_out[0] = None
                    dv1 = p.emit(
                        "dve", lambda e, s1=s1, oc=oc, bk=bk: e.tensor_add(
                            s1, PSF[bk][:, :], af(oc * 512, 512)),
                        waits=[("PE", pe_done), ("B0", bo_ld),
                               ("DV", s1_free[sl2])], inc="DV")
                    psf_free[bk] = ("DV", dv1)
                    p.emit(
                        "dve", lambda e, s1=s1, gc=gc: e.tensor_mul(
                            s1, s1, gc),
                        waits=[(xgsem, xg_ld)], inc="DV")
                    dv3 = p.emit(
                        "dve", lambda e, ot_=ot_, s1=s1, xc=xc: e.tensor_add(
                            ot_, s1, xc),
                        waits=[(f"O{sl2}", ot_free[sl2])], inc="DV")
                    xg_free[sl2] = dv3
                    s1_free[sl2] = dv3

                    def do_out(mt=mt, oc=oc, ot_=ot_, dv3=dv3, sl2=sl2):
                        ot_free[sl2] = p.emit(
                            "sp", lambda e, mt=mt, oc=oc, ot_=ot_:
                            e.dma_start(
                                out=outp[mt * 128:(mt + 1) * 128,
                                         oc * 512:(oc + 1) * 512], in_=ot_),
                            waits=[("DV", dv3)], inc=f"O{sl2}")
                    pending_out[0] = do_out
        if pending_out[0] is not None:
            pending_out[0]()
            pending_out[0] = None
        p.wait("sp", [("O0", c["O0"]), ("O1", c["O1"])])
        return finish()


def _tile_w(wT, ntiles, width):
    K, N = wT.shape
    nkt = K // 128
    return np.ascontiguousarray(
        wT.reshape(nkt, 128, ntiles, width).transpose(2, 1, 0, 3).reshape(
            ntiles, 128, nkt * width))


def kernel(L0_attn_x, L0_attn_Wq, L0_attn_bq, L0_attn_Wk, L0_attn_bk,
           L0_attn_Wv, L0_attn_bv, L0_attn_Wo, L0_attn_bo,
           L0_ffn_gate_w, L0_ffn_up_w, L0_ffn_down_w,
           L0_ffn_gate_scale, L0_ffn_up_scale, L0_ffn_down_scale):
    x = np.asarray(L0_attn_x, dtype=np.float32)
    Wq = np.asarray(L0_attn_Wq, np.float32)
    Wk = np.asarray(L0_attn_Wk, np.float32)
    Wv = np.asarray(L0_attn_Wv, np.float32)
    Wo = np.asarray(L0_attn_Wo, np.float32)
    bq = np.asarray(L0_attn_bq, np.float32)
    bk = np.asarray(L0_attn_bk, np.float32)
    bv = np.asarray(L0_attn_bv, np.float32)
    bo_ = np.asarray(L0_attn_bo, np.float32)
    gw = np.asarray(L0_ffn_gate_w, np.float32)
    uw = np.asarray(L0_ffn_up_w, np.float32)
    dw = np.asarray(L0_ffn_down_w, np.float32)
    gs = np.float32(L0_ffn_gate_scale).reshape(1, 1)
    us = np.float32(L0_ffn_up_scale).reshape(1, 1)
    ds = np.float32(L0_ffn_down_scale).reshape(1, 1)

    ident = np.eye(128, dtype=ml_dtypes.bfloat16)
    identf = np.eye(128, dtype=np.float32)
    mask = np.triu(np.full((128, 128), NEG, np.float32), k=1)

    halves = []
    for r in range(2):
        sl = slice(OH * r, OH * (r + 1))
        wq_h = _tile_w(np.ascontiguousarray(Wq[sl, :].T), HH, 128)
        wk_h = _tile_w(np.ascontiguousarray(Wk[sl, :].T), HH, 128)
        wv_h = _tile_w(np.ascontiguousarray(Wv[sl, :].T), HH, 128)
        bq_h = np.ascontiguousarray(bq[sl].reshape(HH, 128).T)
        bk_h = np.ascontiguousarray(bk[sl].reshape(HH, 128).T)
        bv_h = np.ascontiguousarray(bv[sl].reshape(HH, 128).T)
        gidx_v = np.empty((128, 32), np.int32)
        for t in range(32):
            gidx_v[:, t] = (t * 128 + np.arange(128)) * 2 + r
        halves.append((wq_h, wk_h, wv_h, bq_h, bk_h, bv_h, gidx_v))

    wo_h = _tile_w(np.ascontiguousarray(Wo.T), 8, 512)
    gw_h = _tile_w(np.ascontiguousarray(gw.T), NIT, 128)
    uw_h = _tile_w(np.ascontiguousarray(uw.T), NIT, 128)
    dw_h = _tile_w(np.ascontiguousarray(dw.T), 32, 128)

    in_maps = []
    for cid in range(8):
        b = cid // 2
        r = cid % 2
        wq_h, wk_h, wv_h, bq_h, bk_h, bv_h, gidx_v = halves[r]
        in_maps.append({
            "xa": np.ascontiguousarray(x[b]),
            "xo": np.ascontiguousarray(x[b, MROWS * r:MROWS * (r + 1)]),
            "wq": wq_h, "wk": wk_h, "wv": wv_h, "wo": wo_h,
            "gwh": gw_h, "uwh": uw_h, "dwh": dw_h,
            "bqh": bq_h, "bkh": bk_h, "bvh": bv_h,
            "bo": bo_.reshape(1, D),
            "gsc": gs, "usc": us, "dsc": ds,
            "identw": ident, "identfw": identf, "maskw": mask,
            "gidx": gidx_v,
        })

    nc = build_nc()
    res = run_bass_kernel_spmd(nc, in_maps, core_ids=list(range(8)))
    out = np.empty((NB, SEQ, D), np.float32)
    for cid in range(8):
        b = cid // 2
        r = cid % 2
        out[b, MROWS * r:MROWS * (r + 1), :] = res.results[cid]["outp"]
    return out


# version 16
# speedup vs baseline: 1.1096x; 1.0919x over previous
import math
from contextlib import ExitStack

import numpy as np
import ml_dtypes
import concourse.bass as bass
import concourse.mybir as mybir
from concourse.bass_utils import run_bass_kernel_spmd

F32 = mybir.dt.float32
BF16 = mybir.dt.bfloat16
I32 = mybir.dt.int32
AF = mybir.ActivationFunctionType
ALU = mybir.AluOpType
AX = mybir.AxisListType

D = 4096
SEQ = 2048
NB = 4
HH = 16
OH = HH * 128
NKT = D // 128
NQT = SEQ // 128
INTER = 11008
NIT = INTER // 128
MROWS = 1024
SCALE = 1.0 / math.sqrt(128.0)
NEG = -1.0e9

GW_G = [2048, 1536, 1024, 512]
PT_GOFF = [0, 8192, 14336, 18432]
NGU = NIT * 4
GU_S0 = 8

ENGS = ("pe", "dve", "act", "pool", "sp")
ENG_SEMS = {"PE", "DV", "AC", "CC", "PL"}
SEM_NAMES = (["C0", "X0", "X1", "W0", "W1", "SK0", "SK1", "SV0", "SV1",
              "SG0", "SG1", "K0", "K1", "SA0", "SA1", "GA", "B0",
              "XG0", "XG1", "O0", "O1", "RB0", "RB1", "LS0", "LS1",
              "XD0", "XD1", "XF0", "SI0", "SI1",
              "UW0", "UW1", "IN0", "IN1", "IN2", "IN3",
              "DWC0", "DWS", "XP0", "XP1"]
             + [f"WO{i}" for i in range(12)]
             + ["PE", "DV", "AC", "CC"])


class Plan:
    def __init__(self, sems):
        self.sems = sems
        self.ops = {e: [] for e in ENGS}
        self.cnt = {name: 0 for name in sems}

    def emit(self, eng, fn, waits=(), inc=None):
        sems = self.sems
        waits = list(waits)
        if eng == "dve":
            waits.append(("DV", self.cnt["DV"]))
            if inc is None:
                inc = "DV"
        elif eng == "act":
            waits.append(("AC", self.cnt["AC"]))
            if inc is None:
                inc = "AC"
        wlist = [(sems[s], v) for (s, v) in waits if v > 0]
        if inc is not None:
            by = 1 if inc in ENG_SEMS else 16
            self.cnt[inc] += by
            isem, iby = sems[inc], by
        else:
            isem, iby = None, 0

        def run(e, fn=fn, wlist=wlist, isem=isem, iby=iby):
            for s, v in wlist:
                e.wait_ge(s, v)
            inst = fn(e)
            if isem is not None:
                inst.then_inc(isem, iby)

        self.ops[eng].append(run)
        return self.cnt[inc] if inc is not None else None

    def wait(self, eng, waits):
        sems = self.sems
        wlist = [(sems[s], v) for (s, v) in waits if v > 0]

        def run(e, wlist=wlist):
            for s, v in wlist:
                e.wait_ge(s, v)

        self.ops[eng].append(run)


def build_nc(upto=7, debug=False):
    nc = bass.Bass("TRN2", target_bir_lowering=False)
    dkind = "ExternalOutput" if debug else "Internal"

    xa = nc.dram_tensor("xa", [SEQ, D], F32, kind="ExternalInput").ap()
    xo = nc.dram_tensor("xo", [MROWS, D], F32, kind="ExternalInput").ap()
    wq = nc.dram_tensor("wq", [HH, 128, D], F32, kind="ExternalInput").ap()
    wk = nc.dram_tensor("wk", [HH, 128, D], F32, kind="ExternalInput").ap()
    wv = nc.dram_tensor("wv", [HH, 128, D], F32, kind="ExternalInput").ap()
    wo = nc.dram_tensor("wo", [8, 128, 32 * 512], F32, kind="ExternalInput").ap()
    gwh = nc.dram_tensor("gwh", [NIT, 128, D], F32, kind="ExternalInput").ap()
    uwh = nc.dram_tensor("uwh", [NIT, 128, D], F32, kind="ExternalInput").ap()
    dwh = nc.dram_tensor("dwh", [32, 128, INTER], F32, kind="ExternalInput").ap()
    bqh = nc.dram_tensor("bqh", [128, HH], F32, kind="ExternalInput").ap()
    bkh = nc.dram_tensor("bkh", [128, HH], F32, kind="ExternalInput").ap()
    bvh = nc.dram_tensor("bvh", [128, HH], F32, kind="ExternalInput").ap()
    bo = nc.dram_tensor("bo", [1, D], F32, kind="ExternalInput").ap()
    gsc = nc.dram_tensor("gsc", [1, 1], F32, kind="ExternalInput").ap()
    usc = nc.dram_tensor("usc", [1, 1], F32, kind="ExternalInput").ap()
    dsc = nc.dram_tensor("dsc", [1, 1], F32, kind="ExternalInput").ap()
    identw = nc.dram_tensor("identw", [128, 128], BF16, kind="ExternalInput").ap()
    identfw = nc.dram_tensor("identfw", [128, 128], F32, kind="ExternalInput").ap()
    maskw = nc.dram_tensor("maskw", [128, 128], F32, kind="ExternalInput").ap()
    gidx = nc.dram_tensor("gidx", [128, 32], I32, kind="ExternalInput").ap()
    outp = nc.dram_tensor("outp", [MROWS, D], F32, kind="ExternalOutput").ap()

    K_s = nc.dram_tensor("K_s", [HH, 128, SEQ], BF16, kind=dkind).ap()
    Q_s = nc.dram_tensor("Q_s", [HH, 128, SEQ], BF16, kind=dkind).ap()
    V_s = nc.dram_tensor("V_s", [HH, 128, SEQ], BF16, kind=dkind).ap()
    cc_in = nc.dram_tensor("cc_in", [OH, SEQ], BF16, kind=dkind).ap()
    cc_out = nc.dram_tensor("cc_out", [2 * OH, SEQ], BF16, kind=dkind).ap()
    xn_d = nc.dram_tensor("xn_d", [D, SEQ], BF16, kind=dkind).ap()
    dw_bf = nc.dram_tensor("dw_bf", [32, 128, INTER], BF16, kind="Internal").ap()
    inter_d = nc.dram_tensor("inter_d", [NIT, 128, MROWS], BF16, kind=dkind).ap()
    g_s = nc.dram_tensor("g_s", [MROWS, D], F32, kind=dkind).ap()
    led_d = nc.dram_tensor("led_d", [16, 256], F32, kind="Internal").ap()
    xnd = (nc.dram_tensor("xnd", [128, 65536], BF16, kind="ExternalOutput").ap()
           if debug and upto <= 1 else None)

    with ExitStack() as stack:
        sems = {n: stack.enter_context(nc.semaphore(n)) for n in SEM_NAMES}

        AB = stack.enter_context(nc.sbuf_tensor("AB", [128, 82000], BF16))
        AFR = stack.enter_context(nc.sbuf_tensor("AFR", [128, 9728], F32))
        ident = stack.enter_context(nc.sbuf_tensor("ident", [128, 128], BF16))
        identf = stack.enter_context(nc.sbuf_tensor("identf", [128, 128], F32))
        maskt = stack.enter_context(nc.sbuf_tensor("maskt", [128, 128], F32))
        biasq = stack.enter_context(nc.sbuf_tensor("biasq", [128, HH], F32))
        biask = stack.enter_context(nc.sbuf_tensor("biask", [128, HH], F32))
        biasv = stack.enter_context(nc.sbuf_tensor("biasv", [128, HH], F32))
        svec = stack.enter_context(nc.sbuf_tensor("svec", [128, 8], F32))
        stat = stack.enter_context(nc.sbuf_tensor("stat", [128, 24], F32))
        gidx_sb = stack.enter_context(nc.sbuf_tensor("gidx_sb", [128, 32], I32))
        rrmat = stack.enter_context(nc.sbuf_tensor("rrmat", [128, 32], F32))
        led = stack.enter_context(nc.sbuf_tensor("led", [16, 256], F32))
        rr_war = [0]

        PSF = [stack.enter_context(nc.psum_tensor(f"psf{i}", [128, 512], F32))
               for i in range(6)]
        PSB = [stack.enter_context(nc.psum_tensor(f"psb{i}", [128, 1024], BF16))
               for i in range(2)]

        p = Plan(sems)
        c = p.cnt

        psf_free = [("PE", 0)] * 6
        psb_free = [("PE", 0)] * 2

        def ab(off, n):
            return AB[:, off:off + n]

        def af(off, n):
            return AFR[:, off:off + n]

        for dst_t, src_t in ((ident, identw), (identf, identfw),
                             (maskt, maskw), (biasq, bqh), (biask, bkh),
                             (biasv, bvh), (gidx_sb, gidx)):
            p.emit("sp", lambda e, d=dst_t, s=src_t: e.dma_start(
                out=d[:, :], in_=s), inc="C0")
        for col, src_t in ((0, gsc), (1, usc), (2, dsc)):
            p.emit("sp", lambda e, col=col, s=src_t: e.dma_start(
                out=svec[:, col:col + 1],
                in_=s.partition_broadcast(128)[:, 0, :]), inc="C0")
        consts_ld = c["C0"]
        for eng in ("pe", "dve", "act"):
            p.wait(eng, [("C0", consts_ld)])
        usds_dv = p.emit(
            "dve", lambda e: e.tensor_mul(svec[:, 3:4], svec[:, 1:2],
                                          svec[:, 2:3]), inc="DV")

        def ln_pass(src_ap, nrow_tiles, xn_off, mwidth, xbuf_off, xnrm_off):
            dv0, pe0 = c["DV"], c["PE"]
            xd_done = [0, 0]
            dv_free = [("DV", dv0), ("DV", dv0)]
            nrm_free = [dv0, dv0]
            nrm_pe_free = [pe0, pe0]
            cks = []
            for t in range(nrow_tiles):
                sl = t % 2
                sb = 16 * sl
                xsem = f"X{sl}"
                xb = af(xbuf_off + sl * 4096, 4096)
                xpsem = f"XP{sl}"
                xd_done[sl] = p.emit(
                    "sp",
                    lambda e, src=src_ap, t=t, sl=sl: e.dma_start(
                        out=af(xbuf_off + sl * 4096, 2048),
                        in_=src[t * 128:(t + 1) * 128, 0:2048]),
                    waits=[dv_free[sl]], inc=xsem)
                xp_done = p.emit(
                    "pool",
                    lambda e, src=src_ap, t=t, sl=sl: e.dma_start(
                        out=af(xbuf_off + sl * 4096 + 2048, 2048),
                        in_=src[t * 128:(t + 1) * 128, 2048:4096]),
                    waits=[dv_free[sl]], inc=xpsem)
                xr = ab(xnrm_off + sl * 4096, 4096)
                p.emit("dve", lambda e, xb=xb, sb=sb: e.reduce_sum(
                    stat[:, sb + 0:sb + 1], xb, AX.X),
                    waits=[(xsem, xd_done[sl]), (xpsem, xp_done)])
                ac_sq = p.emit("act", lambda e, xb=xb, xr=xr: e.activation(
                    xr, xb, AF.Square),
                    waits=[(xsem, xd_done[sl]), (xpsem, xp_done),
                           ("DV", nrm_free[sl])], inc="AC")
                p.emit("dve", lambda e, xr=xr, sb=sb: e.reduce_sum(
                    stat[:, sb + 1:sb + 2], xr, AX.X), waits=[("AC", ac_sq)])
                p.emit("dve", lambda e, sb=sb: e.tensor_scalar(
                    stat[:, sb + 2:sb + 3], stat[:, sb + 0:sb + 1],
                    1.0 / D, None, ALU.mult))
                p.emit("dve", lambda e, sb=sb: e.tensor_mul(
                    stat[:, sb + 3:sb + 4], stat[:, sb + 2:sb + 3],
                    stat[:, sb + 2:sb + 3]))
                p.emit("dve", lambda e, sb=sb: e.tensor_scalar(
                    stat[:, sb + 4:sb + 5], stat[:, sb + 1:sb + 2],
                    1.0 / D, 1.0e-5, ALU.mult, ALU.add))
                dv_var = p.emit("dve", lambda e, sb=sb: e.tensor_sub(
                    stat[:, sb + 4:sb + 5], stat[:, sb + 4:sb + 5],
                    stat[:, sb + 3:sb + 4]), inc="DV")
                ac_std = p.emit("act", lambda e, sb=sb: e.activation(
                    stat[:, sb + 5:sb + 6], stat[:, sb + 4:sb + 5], AF.Sqrt),
                    waits=[("DV", dv_var)], inc="AC")
                p.emit("dve", lambda e, sb=sb: e.reciprocal(
                    stat[:, sb + 6:sb + 7], stat[:, sb + 5:sb + 6]),
                    waits=[("AC", ac_std)])
                p.emit("dve", lambda e, sb=sb: e.tensor_mul(
                    stat[:, sb + 7:sb + 8], stat[:, sb + 2:sb + 3],
                    stat[:, sb + 6:sb + 7]))
                dv_nmr = p.emit("dve", lambda e, sb=sb: e.tensor_scalar(
                    stat[:, sb + 7:sb + 8], stat[:, sb + 7:sb + 8],
                    -1.0, None, ALU.mult), inc="DV")
                ac_xn = p.emit("act", lambda e, xb=xb, xr=xr, sb=sb:
                               e.activation(
                                   xr, xb, AF.Identity,
                                   bias=stat[:, sb + 7:sb + 8],
                                   scale=stat[:, sb + 6:sb + 7]),
                               waits=[("DV", dv_nmr),
                                      ("PE", nrm_pe_free[sl])], inc="AC")
                dv_free[sl] = ("AC", ac_xn)
                for g in range(4):
                    bank = g % 2
                    pe_tr = None
                    for j in range(8):
                        kt = g * 8 + j
                        pe_tr = p.emit(
                            "pe",
                            lambda e, xr=xr, kt=kt, j=j, bank=bank: e.transpose(
                                PSB[bank][:, j * 128:(j + 1) * 128],
                                xr[:, kt * 128:(kt + 1) * 128], ident[:, :]),
                            waits=([("AC", ac_xn), psb_free[bank]]
                                   if j == 0 else []),
                            inc=("PE" if j == 7 else None))
                    base = xn_off + (g * 8) * mwidth + t * 128

                    def evac(e, bank=bank, base=base, mw=mwidth):
                        dst = AB[:, base:base + 8 * mw]
                        v = dst.rearrange("p (k m) -> p k m", k=8)[:, :, 0:128]
                        src = PSB[bank][:, :].rearrange("p (k m) -> p k m", k=8)
                        return e.tensor_copy(v, src)
                    dv_ev = p.emit("dve", evac, waits=[("PE", pe_tr)], inc="DV")
                    psb_free[bank] = ("DV", dv_ev)
                nrm_free[sl] = c["DV"]
                nrm_pe_free[sl] = c["PE"]
                if t % 4 == 3:
                    cks.append(c["DV"])
            return cks

        def finish():
            block = stack.enter_context(nc.Block())
            block.tensor(lambda e: [f(e) for f in p.ops["pe"]])
            block.vector(lambda e: [f(e) for f in p.ops["dve"]])
            block.scalar(lambda e: [f(e) for f in p.ops["act"]])
            block.gpsimd(lambda e: [f(e) for f in p.ops["pool"]])
            block.sync(lambda e: [f(e) for f in p.ops["sp"]])
            return nc

        XN_OFF = 0
        XNRM_OFF = 65536
        WBUF_OFF = 73728
        KQSTAGE_OFF = 65536
        VFM_OFF = 66560
        VRM_OFF = 67584
        wd_done = [0, 0]
        p.emit("pool", lambda e: e.dma_start(
            out=ab(WBUF_OFF, 4096), in_=wk[0]), inc="W0")
        wd_done[0] = c["W0"]
        p.emit("pool", lambda e: e.dma_start(
            out=ab(WBUF_OFF + 4096, 4096), in_=wv[0]), inc="W1")
        wd_done[1] = c["W1"]
        p1_cks = ln_pass(xa, NQT, XN_OFF, SEQ, 0, XNRM_OFF)
        p1_dv = p1_cks[-1]
        if upto <= 1:
            if debug:
                xn_st = p.emit("sp", lambda e: e.dma_start(
                    out=xnd, in_=AB[:, 0:65536]), waits=[("DV", p1_dv)],
                    inc="GA")
                p.wait("sp", [("GA", xn_st)])
            return finish()

        wfree = [0, 0]
        kq_free = [0, 0]
        vfm_free = [0, 0]
        vrm_free = [0, 0]
        mats = [(wk, biask, K_s), (wv, biasv, V_s), (wq, biasq, Q_s)]
        for ot in range(HH):
            if 8 <= ot < 16:
                for j2 in range(4):
                    kt = (ot - 8) * 4 + j2
                    ssl = kt % 2
                    p.emit("sp", lambda e, kt=kt: e.dma_start(
                        out=xn_d[kt * 128:(kt + 1) * 128, :],
                        in_=ab(XN_OFF + kt * SEQ, SEQ)),
                        waits=([("DV", p1_dv)] if kt == 0 else []),
                        inc=f"XD{ssl}")
            if 2 <= ot:
                cot = ot - 2
                p.emit("pool", lambda e, cot=cot: e.dma_start(
                    out=dw_bf[cot], in_=dwh[cot]), inc="DWC0")
            for mi, (wmat, bvec, dst) in enumerate(mats):
                it = ot * 3 + mi
                sl = it % 2
                wsem = f"W{sl}"
                wb = ab(WBUF_OFF + sl * 4096, 4096)
                if it >= 2:
                    wd_done[sl] = p.emit(
                        "pool", lambda e, wmat=wmat, ot=ot, wb=wb:
                        e.dma_start(out=wb, in_=wmat[ot]),
                        waits=[("PE", wfree[sl])], inc=wsem)
                vsl = ot % 2
                vr = ab(VRM_OFF + vsl * 2048, 2048)
                v_dv = [None] * 4

                def emit_vtrans(mc):
                    ssl = mc % 2
                    vf = ab(VFM_OFF + ssl * 512, 512)
                    bank = mc % 2
                    pe_tr = None
                    for ms in range(4):
                        pe_tr = p.emit(
                            "pe",
                            lambda e, vf=vf, ms=ms, bank=bank: e.transpose(
                                PSB[bank][:, ms * 128:(ms + 1) * 128],
                                vf[:, ms * 128:(ms + 1) * 128],
                                ident[:, :]),
                            waits=([v_dv[mc], psb_free[bank]]
                                   if ms == 0 else []),
                            inc=("PE" if ms == 3 else None))
                    vfm_free[ssl] = pe_tr
                    dv_ev = p.emit(
                        "dve",
                        lambda e, vr=vr, mc=mc, bank=bank: e.tensor_copy(
                            vr[:, mc * 512:(mc + 1) * 512],
                            PSB[bank][:, 0:512]),
                        waits=([("PE", pe_tr), (f"SV{vsl}", vrm_free[vsl])]
                               if mc == 0 else [("PE", pe_tr)]),
                        inc="DV")
                    psb_free[bank] = ("DV", dv_ev)

                for mc in range(4):
                    for kt in range(NKT):
                        wl = []
                        if kt == 0:
                            wl.append(psf_free[mc])
                            if mc == 0:
                                wl.append((wsem, wd_done[sl]))
                            if it == 0:
                                wl.append(("DV", p1_cks[mc]))
                        p.emit(
                            "pe",
                            lambda e, wb=wb, kt=kt, mc=mc: e.matmul(
                                PSF[mc][:, :],
                                wb[:, kt * 128:(kt + 1) * 128],
                                ab(XN_OFF + kt * SEQ + mc * 512, 512),
                                start=(kt == 0), stop=(kt == NKT - 1),
                                skip_group_check=True),
                            waits=wl,
                            inc=("PE" if kt == NKT - 1 else None))
                    pe_mc = c["PE"]
                    if mi != 1:
                        ssl = mc % 2
                        st_ = ab(KQSTAGE_OFF + ssl * 512, 512)
                        dv_e = p.emit(
                            "dve",
                            lambda e, mc=mc, st_=st_, bvec=bvec, ot=ot:
                            e.tensor_scalar(st_, PSF[mc][:, :],
                                            bvec[:, ot:ot + 1], None,
                                            ALU.add),
                            waits=[("PE", pe_mc),
                                   (f"SK{ssl}", kq_free[ssl])],
                            inc="DV")
                        psf_free[mc] = ("DV", dv_e)
                        kq_free[ssl] = p.emit(
                            "sp",
                            lambda e, dst=dst, ot=ot, mc=mc, st_=st_:
                            e.dma_start(
                                out=dst[ot][:, mc * 512:(mc + 1) * 512],
                                in_=st_),
                            waits=[("DV", dv_e)], inc=f"SK{ssl}")
                    else:
                        ssl = mc % 2
                        vf = ab(VFM_OFF + ssl * 512, 512)
                        dv_e = p.emit(
                            "dve",
                            lambda e, mc=mc, vf=vf, ot=ot: e.tensor_scalar(
                                vf, PSF[mc][:, :], biasv[:, ot:ot + 1],
                                None, ALU.add),
                            waits=[("PE", pe_mc), ("PE", vfm_free[ssl])],
                            inc="DV")
                        psf_free[mc] = ("DV", dv_e)
                        v_dv[mc] = ("DV", dv_e)
                        if mc > 0:
                            emit_vtrans(mc - 1)
                if mi == 1:
                    emit_vtrans(3)
                    vrm_free[vsl] = p.emit(
                        "sp",
                        lambda e, ot=ot, vr=vr: e.dma_start(
                            out=V_s[ot], in_=vr),
                        waits=[("DV", c["DV"])], inc=f"SV{vsl}")
                wfree[sl] = c["PE"]
        p2_stores = [("SK0", c["SK0"]), ("SK1", c["SK1"]),
                     ("SV0", c["SV0"]), ("SV1", c["SV1"])]
        p2_pe = c["PE"]
        p2sk = (c["SK0"], c["SK1"])
        xd_all = [("XD0", c["XD0"]), ("XD1", c["XD1"])]
        if upto <= 2:
            p.wait("sp", p2_stores + xd_all)
            return finish()

        XNF_OFF = 0
        GWB_OFF = 32768
        UWB_OFF = 36864
        ISTG_OFF = 40960
        KB_OFF = 43008
        VB_OFF = 47104
        QB_OFF = 51200
        PR_OFF = 55296
        PT_OFF = 59392
        AT_OFF = 79872
        RB_OFF = 8192
        xn_g = xn_d.rearrange("d (h m) -> (d h) m", h=2)
        for t in range(32):
            wl = ([("PE", p2_pe)] + xd_all) if t == 0 else []
            p.emit(
                "pool", lambda e, t=t: e.indirect_dma_start(
                    out=ab(XNF_OFF + t * 1024, 1024),
                    out_offset=None,
                    in_=xn_g,
                    in_offset=bass.IndirectOffsetOnAxis(
                        gidx_sb[:, t:t + 1], 0)),
                waits=wl, inc="XF0")
        xf_all = c["XF0"]

        gwc = {}
        uwc = {}
        gu = {"next": 0, "gw_free": p2_pe, "uw_free": p2_pe,
              "g0_pe": 0, "sil_ac": 0}
        ist_free = [0, 0]
        first_mul = [True]

        def emit_gw_load(it):
            sl = it % 2
            wl = ([("PE", p2_pe)] + xd_all if it == 0
                  else [("PE", gu["gw_free"])])
            p.emit("pool", lambda e, it=it: e.dma_start(
                out=ab(GWB_OFF, 4096), in_=gwh[it]), waits=wl, inc=f"W{sl}")
            gwc[it] = c[f"W{sl}"]

        def emit_uw_load(it):
            sl = it % 2
            wl = ([("PE", p2_pe)] + xd_all if it == 0
                  else [("PE", gu["uw_free"])])
            p.emit("pool", lambda e, it=it: e.dma_start(
                out=ab(UWB_OFF, 4096), in_=uwh[it]), waits=wl, inc=f"UW{sl}")
            uwc[it] = c[f"UW{sl}"]

        def emit_gu_group(gi):
            it, j = gi // 4, gi % 4
            sl = it % 2
            bk = 2 + (j % 2)
            woff = GWB_OFF if j < 2 else UWB_OFF
            mc = j % 2
            wl0 = [psf_free[bk]]
            if j == 0:
                wl0.append((f"W{sl}", gwc[it]))
                if it == 0:
                    wl0.append(("XF0", xf_all))
            elif j == 2:
                wl0.append((f"UW{sl}", uwc[it]))
            for kt in range(NKT):
                p.emit(
                    "pe", lambda e, kt=kt, mc=mc, bk=bk, woff=woff: e.matmul(
                        PSF[bk][:, :],
                        AB[:, woff + kt * 128:woff + (kt + 1) * 128],
                        ab(XNF_OFF + kt * MROWS + mc * 512, 512),
                        start=(kt == 0), stop=(kt == NKT - 1),
                        skip_group_check=True),
                    waits=(wl0 if kt == 0 else []),
                    inc=("PE" if kt == NKT - 1 else None))
            gpe = c["PE"]
            gu["end_pe"] = gpe
            if j == 0:
                gu["g0_pe"] = gpe
            elif j == 1:
                gu["gw_free"] = gpe
                if it + 1 < NIT:
                    emit_gw_load(it + 1)
                for mc2, bkk in ((0, 2), (1, 3)):
                    dv1 = p.emit(
                        "dve", lambda e, mc2=mc2, bkk=bkk: e.tensor_scalar(
                            af(4096 + mc2 * 512, 512), PSF[bkk][:, :],
                            svec[:, 0:1], None, ALU.mult),
                        waits=[("PE", gpe if mc2 else gu["g0_pe"]),
                               ("DV", usds_dv), ("AC", gu["sil_ac"])],
                        inc="DV")
                    psf_free[bkk] = ("DV", dv1)
                gu["sil_ac"] = p.emit(
                    "act", lambda e: e.activation(
                        af(5120, 1024), af(4096, 1024), AF.Silu),
                    waits=[("DV", c["DV"])], inc="AC")
            else:
                mc2 = j - 2
                wlm = [("AC", gu["sil_ac"]), ("PE", gpe),
                       (f"SI{sl}", ist_free[sl])]
                if first_mul[0]:
                    wlm += xd_all
                    first_mul[0] = False
                dv2 = p.emit(
                    "dve", lambda e, it=it, mc2=mc2, bk=bk: e.tensor_mul(
                        ab(ISTG_OFF + (it % 2) * 1024 + mc2 * 512, 512),
                        af(5120 + mc2 * 512, 512), PSF[bk][:, :]),
                    waits=wlm, inc="DV")
                psf_free[bk] = ("DV", dv2)
                if j == 3:
                    gu["uw_free"] = gpe
                    if it + 1 < NIT:
                        emit_uw_load(it + 1)
                    dvm = c["DV"]
                    ist_free[sl] = p.emit(
                        "pool", lambda e, it=it: e.dma_start(
                            out=inter_d[it],
                            in_=ab(ISTG_OFF + (it % 2) * 1024, 1024)),
                        waits=[("DV", dvm)], inc=f"SI{sl}")
                    if it % 4 == 2 and it < 72:
                        cot = 14 + (it - 2) // 4
                        p.emit("pool", lambda e, cot=cot: e.dma_start(
                            out=dw_bf[cot], in_=dwh[cot]), inc="DWC0")

        emit_gw_load(0)
        emit_uw_load(0)

        def pt_col(t, q):
            g = t // 4
            return PT_OFF + PT_GOFF[g] + (t % 4) * GW_G[g] + (q - 4 * g) * 128

        kv_free = [p2_pe, p2_pe]
        at_free = [0, 0]
        pr_free = [0, 0]
        sc_free = [("AC", 0), ("AC", 0)]
        rb_free = [0, 0]
        led_st = [0, 0]
        rr_tr = [0, 0]
        prev_pt = ("DV", 0)
        prev_vb = None
        led_dv = [0, 0]
        first_pt = [True]
        first_exp = [True]
        p.wait("sp", p2_stores)

        def emit_pv(h, vb, pt_mark):
            for mc in range(4):
                bk = 4 + mc % 2
                ntk = 4 * mc + 4
                rsl = mc % 2
                rb = af(RB_OFF + rsl * 512, 512)
                rb3 = rb.rearrange("p (j c) -> p j c", j=4)
                rb_ld = p.emit(
                    "sp", lambda e, h=h, mc=mc, rb3=rb3: e.dma_start(
                        out=rb3,
                        in_=led_d[4 * mc:4 * mc + 4,
                                  (h % 2) * 128:(h % 2 + 1) * 128]
                        .partition_broadcast(128)),
                    waits=[(f"LS{h % 2}", led_st[h % 2]), ("DV", rb_free[rsl])],
                    inc=f"RB{rsl}")
                pe_pv = None
                for t in range(ntk):
                    off = max(0, t * 128 - mc * 512)
                    nn = 512 - off
                    wl = []
                    if t == 0:
                        wl.append(pt_mark)
                        wl.append(psf_free[bk])
                    pe_pv = p.emit(
                        "pe", lambda e, vb=vb, t=t, mc=mc, off=off, nn=nn,
                        bk=bk: e.matmul(
                            PSF[bk][:, off:off + nn],
                            vb[:, t * 128:(t + 1) * 128],
                            ab(pt_col(t, 4 * mc) + off, nn),
                            start=(t == 0), stop=(t == ntk - 1),
                            skip_group_check=True),
                        waits=wl, inc=("PE" if t == ntk - 1 else None))
                asl = mc % 2
                at = ab(AT_OFF + asl * 512, 512)
                dv_a = p.emit(
                    "dve", lambda e, at=at, bk=bk, rb=rb: e.tensor_mul(
                        at, PSF[bk][:, :], rb),
                    waits=[("PE", pe_pv), (f"RB{rsl}", rb_ld),
                           (f"SA{asl}", at_free[asl])],
                    inc="DV")
                psf_free[bk] = ("DV", dv_a)
                rb_free[rsl] = dv_a
                at_free[asl] = p.emit(
                    "sp", lambda e, h=h, mc=mc, at=at: e.dma_start(
                        out=cc_in[h * 128:(h + 1) * 128,
                                  mc * 512:(mc + 1) * 512], in_=at),
                    waits=[("DV", dv_a)], inc=f"SA{asl}")
            return c["PE"]

        ac_e_mark = {}
        bias_mark = {}
        for h in range(HH):
            sl = h % 2
            ksem = f"K{sl}"
            kb = ab(KB_OFF + sl * 2048, 2048)
            vb = ab(VB_OFF + sl * 2048, 2048)
            qb = ab(QB_OFF + sl * 2048, 2048)
            wl0 = [("PE", kv_free[sl])]
            if h == 0:
                wl0 += p2_stores + xd_all
            p.emit("sp", lambda e, h=h, kb=kb: e.dma_start(
                out=kb, in_=K_s[h]), waits=wl0, inc=ksem)
            p.emit("sp", lambda e, h=h, vb=vb: e.dma_start(
                out=vb, in_=V_s[h]), inc=ksem)
            kd_done = p.emit("sp", lambda e, h=h, qb=qb: e.dma_start(
                out=qb, in_=Q_s[h]), inc=ksem)

            for i in range(NQT + 2):
                if i < NQT:
                    qt = i
                    Kt = 128 * (qt + 1)
                    nkc = (Kt + 511) // 512
                    psl = qt % 2
                    sc = af(psl * 2048, 2048)
                    sc_ac = []
                    for kc in range(nkc):
                        nn = min(512, Kt - kc * 512)
                        bk = kc % 2
                        wl = [psf_free[bk]]
                        if qt == 0 and kc == 0:
                            wl.append((ksem, kd_done))
                        pe_s = p.emit(
                            "pe", lambda e, qb=qb, kb=kb, qt=qt, kc=kc,
                            nn=nn, bk=bk: e.matmul(
                                PSF[bk][:, 0:nn],
                                qb[:, qt * 128:(qt + 1) * 128],
                                kb[:, kc * 512:kc * 512 + nn],
                                start=True, stop=True,
                                skip_group_check=True),
                            waits=wl, inc="PE")
                        wl2 = [("PE", pe_s)]
                        if kc <= 1:
                            wl2.append(sc_free[psl])
                        if kc % 2 == 0:
                            ev = p.emit(
                                "act", lambda e, sc=sc, kc=kc, nn=nn, bk=bk:
                                e.copy(sc[:, kc * 512:kc * 512 + nn],
                                       PSF[bk][:, 0:nn]),
                                waits=wl2, inc="AC")
                            psf_free[bk] = ("AC", ev)
                            sc_ac.append(("AC", ev))
                        else:
                            ev = p.emit(
                                "dve", lambda e, sc=sc, kc=kc, nn=nn, bk=bk:
                                e.tensor_copy(sc[:, kc * 512:kc * 512 + nn],
                                              PSF[bk][:, 0:nn]),
                                waits=wl2, inc="DV")
                            psf_free[bk] = ("DV", ev)
                            sc_ac.append(("DV", ev))
                if i - 2 >= 0:
                    pqt = i - 2
                    pac = ac_e_mark[(h, pqt)]
                    ntile = pqt + 1
                    for g0 in range(0, ntile, 4):
                        g = g0 // 4
                        gsz = min(4, ntile - g0)
                        bank = g % 2
                        pe_tr = None
                        for j in range(gsz):
                            t = g0 + j
                            pe_tr = p.emit(
                                "pe", lambda e, pr2=ab(
                                    PR_OFF + (pqt % 2) * 2048, 2048), t=t,
                                j=j, bank=bank: e.transpose(
                                    PSB[bank][:, j * 128:(j + 1) * 128],
                                    pr2[:, t * 128:(t + 1) * 128],
                                    ident[:, :]),
                                waits=([("AC", pac), psb_free[bank]]
                                       if j == 0 else []),
                                inc=("PE" if j == gsz - 1 else None))
                        base = pt_col(4 * g, pqt)
                        gww = GW_G[g]

                        def evacp(e, base=base, gsz=gsz, bank=bank, gww=gww):
                            dst = AB[:, base:base + gsz * gww]
                            v = dst.rearrange("p (k m) -> p k m",
                                              k=gsz)[:, :, 0:128]
                            src = PSB[bank][:, 0:gsz * 128].rearrange(
                                "p (k m) -> p k m", k=gsz)
                            return e.tensor_copy(v, src)
                        wle = [("PE", pe_tr)]
                        if first_pt[0]:
                            wle += xd_all
                            first_pt[0] = False
                        dv_pt = p.emit("dve", evacp, waits=wle, inc="DV")
                        psb_free[bank] = ("DV", dv_pt)
                    pr_free[pqt % 2] = c["PE"]
                if h > 0 and i == 0:
                    kv_free[1 - sl] = emit_pv(h - 1, prev_vb, prev_pt)
                if i < NQT:
                    qt = i
                    Kt = 128 * (qt + 1)
                    psl = qt % 2
                    sc = af(psl * 2048, 2048)
                    dv_m = p.emit(
                        "dve", lambda e, sc=sc, Kt=Kt: e.tensor_add(
                            sc[:, Kt - 128:Kt], sc[:, Kt - 128:Kt],
                            maskt[:, :]),
                        waits=sc_ac, inc="DV")
                    p.emit("dve", lambda e, sc=sc, Kt=Kt, psl=psl:
                           e.reduce_max(stat[:, 8 + psl:9 + psl],
                                        sc[:, 0:Kt], AX.X, negate=True),
                           waits=[("DV", dv_m)])
                    bias_mark[(h, qt)] = p.emit(
                        "dve", lambda e, psl=psl: e.tensor_scalar(
                            stat[:, 10 + psl:11 + psl],
                            stat[:, 8 + psl:9 + psl], SCALE, None, ALU.mult),
                        waits=[("AC", ac_e_mark.get((h, qt - 2), 0))],
                        inc="DV")
                if i - 1 >= 0 and i - 1 < NQT:
                    qt = i - 1
                    Kt = 128 * (qt + 1)
                    psl = qt % 2
                    sc = af(psl * 2048, 2048)
                    pr = ab(PR_OFF + psl * 2048, 2048)
                    wle = [("DV", bias_mark[(h, qt)]),
                           ("PE", pr_free[psl]),
                           ("DV", rr_war[0])]
                    if first_exp[0]:
                        wle += xd_all
                        first_exp[0] = False
                    ac_e = p.emit(
                        "act", lambda e, sc=sc, pr=pr, Kt=Kt, psl=psl:
                        e.activation(
                            pr[:, 0:Kt], sc[:, 0:Kt], AF.Exp,
                            bias=stat[:, 10 + psl:11 + psl], scale=SCALE,
                            accum_out=stat[:, 12 + psl:13 + psl]),
                        waits=wle, inc="AC")
                    ac_e_mark[(h, qt)] = ac_e
                    sc_free[psl] = ("AC", ac_e)
                    rr_war[0] = p.emit(
                        "dve", lambda e, h=h, qt=qt, psl=psl: e.reciprocal(
                            rrmat[:, (h % 2) * 16 + qt:
                                  (h % 2) * 16 + qt + 1],
                            stat[:, 12 + psl:13 + psl]),
                        waits=[("AC", ac_e), ("PE", rr_tr[h % 2])],
                        inc="DV")
                if i < NQT:
                    s_glob = h * NQT + i
                    tgt = (0 if s_glob < GU_S0 else
                           min(NGU, (s_glob - GU_S0 + 1) * NGU
                               // (HH * NQT - GU_S0)))
                    while gu["next"] < tgt:
                        emit_gu_group(gu["next"])
                        gu["next"] += 1
            pe_rr = p.emit(
                "pe", lambda e, h=h: e.transpose(
                    PSF[3][0:16, 0:128],
                    rrmat[:, (h % 2) * 16:(h % 2) * 16 + 16],
                    identf[:, :]),
                waits=[("DV", rr_war[0]), psf_free[3]], inc="PE")
            rr_tr[h % 2] = pe_rr
            led_dv[h % 2] = p.emit(
                "dve", lambda e, h=h: e.tensor_copy(
                    led[0:16, (h % 2) * 128:(h % 2 + 1) * 128],
                    PSF[3][0:16, 0:128]),
                waits=[("PE", pe_rr), (f"LS{h % 2}", led_st[h % 2])],
                inc="DV")
            psf_free[3] = ("DV", led_dv[h % 2])
            led_st[h % 2] = p.emit(
                "sp", lambda e, h=h: e.dma_start(
                    out=led_d[:, (h % 2) * 128:(h % 2 + 1) * 128],
                    in_=led[0:16, (h % 2) * 128:(h % 2 + 1) * 128]),
                waits=[("DV", led_dv[h % 2])], inc=f"LS{h % 2}")
            prev_pt = ("DV", c["DV"])
            prev_vb = vb
        kv_free[(HH - 1) % 2] = emit_pv(HH - 1, prev_vb, prev_pt)
        while gu["next"] < NGU:
            emit_gu_group(gu["next"])
            gu["next"] += 1
        att_stores = [("SA0", c["SA0"]), ("SA1", c["SA1"])]
        si_all = [("SI0", c["SI0"]), ("SI1", c["SI1"])]
        p4_pe = c["PE"]
        cc_g = cc_out.rearrange("d (b m) -> (d b) m", b=2)
        if upto <= 4:
            p.wait("sp", att_stores)
            p.wait("pool", si_all + [("XF0", xf_all)])
            return finish()

        DW2_OFF = 44032

        def int_col(mch, il):
            if il < 30:
                return (66560 if mch == 0 else 28672) + il * 512
            return (il - 30) * 512

        def emit_int_loads(mch, il0, il1, wl_first):
            first = [True, True]
            for it in range(il0, il1):
                q = it % 2
                isl = q + 2 * mch
                wl = wl_first if first[q] else []
                first[q] = False
                p.emit("pool" if q == 0 else "sp",
                       lambda e, it=it, mch=mch: e.dma_start(
                           out=ab(int_col(mch, it), 512),
                           in_=inter_d[it][:, mch * 512:(mch + 1) * 512]),
                       waits=wl, inc=f"IN{isl}")

        DWSEM = ("W0", "DWS")
        dwc_tot = c["DWC0"]

        def emit_dw_load(ot, sl2, first=False):
            wl = [("PE", dw_free[sl2])]
            if first:
                wl.append(("DWC0", dwc_tot))
            p.emit("pool" if sl2 == 0 else "sp",
                   lambda e, ot=ot, sl2=sl2: e.dma_start(
                       out=AB[:, DW2_OFF + sl2 * INTER:
                              DW2_OFF + sl2 * INTER + INTER],
                       in_=dw_bf[ot]),
                   waits=wl, inc=DWSEM[sl2])
            dwc[sl2] = c[DWSEM[sl2]]

        dw_free = [p4_pe, p4_pe]
        dwc = [0, 0]
        fs_free = [("PE", 0), ("PE", 0)]
        gst_free = [0, 0]
        chunk_end = p4_pe
        cc_done = None
        ao_done1 = 0
        g_idx = 0
        for mch in range(2):
            if mch == 0:
                emit_int_loads(0, 30, NIT,
                               si_all + [("PE", gu["end_pe"])])
                emit_int_loads(0, 0, 30, [("PE", p4_pe)] + att_stores)
                in_tot = [("IN0", c["IN0"]), ("IN1", c["IN1"])]
                emit_dw_load(0, 0, first=True)
                emit_dw_load(1, 1, first=True)
                emit_int_loads(1, 0, 30, si_all + [("PE", p4_pe)])
                cc_done = p.emit(
                    "pool", lambda e: e.collective_compute(
                        "AllGather", ALU.bypass,
                        [[0, 1], [2, 3], [4, 5], [6, 7]],
                        [cc_in.opt()], [cc_out.opt()]),
                    waits=att_stores, inc="CC")
            else:
                emit_dw_load(0, 0)
                emit_int_loads(1, 30, NIT, [("PE", chunk_end)])
                emit_dw_load(1, 1)
                in_tot = [("IN2", c["IN2"]), ("IN3", c["IN3"])]
            for ot in range(32):
                sl2 = ot % 2
                if ot >= 2:
                    emit_dw_load(ot, sl2)
                if mch == 1 and ot == 8:
                    for t in range(15):
                        wl = ([("CC", cc_done), ("PE", chunk_end)]
                              if t == 0 else [])
                        p.emit(
                            "pool", lambda e, t=t: e.indirect_dma_start(
                                out=ab(66560 + t * 1024, 1024),
                                out_offset=None,
                                in_=cc_g,
                                in_offset=bass.IndirectOffsetOnAxis(
                                    gidx_sb[:, t:t + 1], 0)),
                            waits=wl, inc="GA")
                    ao_done1 = c["GA"]
                bk = (2 * g_idx) % 6
                bk2 = (2 * g_idx + 1) % 6
                g_idx += 1
                dwb = DW2_OFF + sl2 * INTER
                for il in range(NIT):
                    wl2 = []
                    if il == 0:
                        wl2 = [psf_free[bk], (DWSEM[sl2], dwc[sl2])]
                        if ot == 0:
                            wl2 += in_tot
                    p.emit(
                        "pe", lambda e, il=il, dwb=dwb, mch=mch, bk=bk:
                        e.matmul(
                            PSF[bk][:, :],
                            AB[:, dwb + il * 128:dwb + (il + 1) * 128],
                            ab(int_col(mch, il), 512),
                            start=(il == 0), stop=(il == NIT - 1),
                            skip_group_check=True),
                        waits=wl2, inc=("PE" if il == NIT - 1 else None))
                pe_d = c["PE"]
                dw_free[sl2] = pe_d
                fs = af(sl2 * 512, 512)
                dv_fs = p.emit(
                    "dve", lambda e, fs=fs, bk=bk: e.tensor_scalar(
                        fs, PSF[bk][:, :], svec[:, 3:4], None, ALU.mult),
                    waits=[("PE", pe_d), ("DV", usds_dv), fs_free[sl2]],
                    inc="DV")
                psf_free[bk] = ("DV", dv_fs)
                pe_tr = None
                for j in range(4):
                    pe_tr = p.emit(
                        "pe", lambda e, fs=fs, j=j, bk2=bk2: e.transpose(
                            PSF[bk2][:, j * 128:(j + 1) * 128],
                            fs[:, j * 128:(j + 1) * 128], identf[:, :]),
                        waits=([("DV", dv_fs), psf_free[bk2]]
                               if j == 0 else []),
                        inc=("PE" if j == 3 else None))
                fs_free[sl2] = ("PE", pe_tr)
                gst = af(1024 + sl2 * 512, 512)
                ac_g = p.emit(
                    "act", lambda e, gst=gst, bk2=bk2: e.activation(
                        gst, PSF[bk2][:, :], AF.Silu),
                    waits=[("PE", pe_tr), (f"SG{sl2}", gst_free[sl2])],
                    inc="AC")
                psf_free[bk2] = ("AC", ac_g)
                gst_free[sl2] = p.emit(
                    "act", lambda e, mch=mch, ot=ot, gst=gst: e.dma_start(
                        out=g_s.rearrange("(ms pp) d -> pp ms d", pp=128)
                        [:, 4 * mch:4 * mch + 4, ot * 128:(ot + 1) * 128],
                        in_=gst.rearrange("pp (ms oc) -> pp ms oc", ms=4)),
                    waits=[("AC", ac_g)], inc=f"SG{sl2}")
            chunk_end = c["PE"]
        down_end = c["PE"]
        g_stores = [("SG0", c["SG0"]), ("SG1", c["SG1"])]
        if upto <= 5:
            p.wait("sp", g_stores)
            p.wait("pool", [("CC", c["CC"]), ("GA", c["GA"])])
            return finish()

        WO_OFF = 17408

        def ao_col(dt):
            return 66560 + dt * 1024 if dt < 15 else (dt - 15) * 1024

        ao_all = None
        bo_ld = None
        NCHUNK = 12
        wo_done = [0] * NCHUNK
        wo_free = [down_end] * NCHUNK
        XC_OFF = 4096
        GC_OFF = 5120
        S1_OFF = 6144
        OT_OFF = 7168
        xg_free = [0, 0]
        s1_free = [0, 0]
        ot_free = [0, 0]
        pending_out = [None]
        chunk_id = 0
        for oc in range(8):
            for dg in range(8):
                ci = chunk_id % NCHUNK
                wo_done[ci] = p.emit(
                    "pool", lambda e, oc=oc, dg=dg, ci=ci: e.dma_start(
                        out=ab(WO_OFF + ci * 2048, 2048),
                        in_=wo[oc][:, dg * 2048:(dg + 1) * 2048]),
                    waits=[("PE", wo_free[ci])],
                    inc=f"WO{ci}")
                chunk_id += 1
                if oc == 0 and dg == 2:
                    for t in range(15, 32):
                        wl = ([("CC", cc_done), ("PE", down_end),
                               ("GA", ao_done1)] if t == 15 else [])
                        p.emit(
                            "pool", lambda e, t=t: e.indirect_dma_start(
                                out=ab(ao_col(t), 1024),
                                out_offset=None,
                                in_=cc_g,
                                in_offset=bass.IndirectOffsetOnAxis(
                                    gidx_sb[:, t:t + 1], 0)),
                            waits=wl, inc="GA")
                    ao_all = c["GA"]
            if oc == 0:
                bo_ld = p.emit("sp", lambda e: e.dma_start(
                    out=af(0, D), in_=bo.partition_broadcast(128)[:, 0, :]),
                    waits=[("GA", ao_all)] + g_stores, inc="B0")
            base_chunk = chunk_id - 8
            for gpi, mts in enumerate(((0, 1, 2), (3, 4, 5), (6, 7))):
                banks = {mt: (gpi * 3 + i) % 6 for i, mt in enumerate(mts)}
                for dt in range(32):
                    ci = (base_chunk + dt // 4) % NCHUNK
                    dl = dt % 4
                    for mt in mts:
                        bk = banks[mt]
                        wl = []
                        if dt == 0:
                            wl.append(psf_free[bk])
                            if mt == mts[0]:
                                wl.append(("GA", ao_done1))
                        if dt == 15 and mt == mts[0]:
                            wl.append(("GA", ao_all))
                        if dl == 0 and mt == mts[0]:
                            wl.append((f"WO{ci}", wo_done[ci]))
                        p.emit(
                            "pe", lambda e, dt=dt, mt=mt, ci=ci, dl=dl,
                            bk=bk: e.matmul(
                                PSF[bk][:, :],
                                ab(ao_col(dt) + mt * 128, 128),
                                ab(WO_OFF + ci * 2048 + dl * 512, 512),
                                start=(dt == 0), stop=(dt == 31),
                                skip_group_check=True),
                            waits=wl,
                            inc=("PE" if (dt == 31 and mt == mts[-1])
                                 else None))
                pe_done = c["PE"]
                if gpi == 2:
                    for k in range(8):
                        wo_free[(base_chunk + k) % NCHUNK] = pe_done
                for mt in mts:
                    bk = banks[mt]
                    sl2 = mt % 2
                    xgsem = f"XG{sl2}"
                    xc = af(XC_OFF + sl2 * 512, 512)
                    gc = af(GC_OFF + sl2 * 512, 512)
                    s1 = af(S1_OFF + sl2 * 512, 512)
                    ot_ = af(OT_OFF + sl2 * 512, 512)
                    p.emit(
                        "sp", lambda e, mt=mt, oc=oc, xc=xc: e.dma_start(
                            out=xc, in_=xo[mt * 128:(mt + 1) * 128,
                                           oc * 512:(oc + 1) * 512]),
                        waits=([("DV", xg_free[sl2])] +
                               ([("GA", ao_all), ("PE", down_end)]
                                if oc == 0 else [])),
                        inc=xgsem)
                    xg_ld = p.emit(
                        "sp", lambda e, mt=mt, oc=oc, gc=gc: e.dma_start(
                            out=gc, in_=g_s[mt * 128:(mt + 1) * 128,
                                            oc * 512:(oc + 1) * 512]),
                        waits=(g_stores if oc == 0 else []), inc=xgsem)
                    if pending_out[0] is not None:
                        pending_out[0]()
                        pending_out[0] = None
                    dv1 = p.emit(
                        "dve", lambda e, s1=s1, oc=oc, bk=bk: e.tensor_add(
                            s1, PSF[bk][:, :], af(oc * 512, 512)),
                        waits=[("PE", pe_done), ("B0", bo_ld),
                               ("DV", s1_free[sl2])], inc="DV")
                    psf_free[bk] = ("DV", dv1)
                    p.emit(
                        "dve", lambda e, s1=s1, gc=gc: e.tensor_mul(
                            s1, s1, gc),
                        waits=[(xgsem, xg_ld)], inc="DV")
                    dv3 = p.emit(
                        "dve", lambda e, ot_=ot_, s1=s1, xc=xc: e.tensor_add(
                            ot_, s1, xc),
                        waits=[(f"O{sl2}", ot_free[sl2])], inc="DV")
                    xg_free[sl2] = dv3
                    s1_free[sl2] = dv3

                    def do_out(mt=mt, oc=oc, ot_=ot_, dv3=dv3, sl2=sl2):
                        ot_free[sl2] = p.emit(
                            "sp", lambda e, mt=mt, oc=oc, ot_=ot_:
                            e.dma_start(
                                out=outp[mt * 128:(mt + 1) * 128,
                                         oc * 512:(oc + 1) * 512], in_=ot_),
                            waits=[("DV", dv3)], inc=f"O{sl2}")
                    pending_out[0] = do_out
        if pending_out[0] is not None:
            pending_out[0]()
            pending_out[0] = None
        p.wait("sp", [("O0", c["O0"]), ("O1", c["O1"])])
        return finish()


def _tile_w(wT, ntiles, width):
    K, N = wT.shape
    nkt = K // 128
    return np.ascontiguousarray(
        wT.reshape(nkt, 128, ntiles, width).transpose(2, 1, 0, 3).reshape(
            ntiles, 128, nkt * width))


def kernel(L0_attn_x, L0_attn_Wq, L0_attn_bq, L0_attn_Wk, L0_attn_bk,
           L0_attn_Wv, L0_attn_bv, L0_attn_Wo, L0_attn_bo,
           L0_ffn_gate_w, L0_ffn_up_w, L0_ffn_down_w,
           L0_ffn_gate_scale, L0_ffn_up_scale, L0_ffn_down_scale):
    x = np.asarray(L0_attn_x, dtype=np.float32)
    Wq = np.asarray(L0_attn_Wq, np.float32)
    Wk = np.asarray(L0_attn_Wk, np.float32)
    Wv = np.asarray(L0_attn_Wv, np.float32)
    Wo = np.asarray(L0_attn_Wo, np.float32)
    bq = np.asarray(L0_attn_bq, np.float32)
    bk = np.asarray(L0_attn_bk, np.float32)
    bv = np.asarray(L0_attn_bv, np.float32)
    bo_ = np.asarray(L0_attn_bo, np.float32)
    gw = np.asarray(L0_ffn_gate_w, np.float32)
    uw = np.asarray(L0_ffn_up_w, np.float32)
    dw = np.asarray(L0_ffn_down_w, np.float32)
    gs = np.float32(L0_ffn_gate_scale).reshape(1, 1)
    us = np.float32(L0_ffn_up_scale).reshape(1, 1)
    ds = np.float32(L0_ffn_down_scale).reshape(1, 1)

    ident = np.eye(128, dtype=ml_dtypes.bfloat16)
    identf = np.eye(128, dtype=np.float32)
    mask = np.triu(np.full((128, 128), NEG, np.float32), k=1)

    halves = []
    for r in range(2):
        sl = slice(OH * r, OH * (r + 1))
        wq_h = _tile_w(np.ascontiguousarray(Wq[sl, :].T), HH, 128)
        wk_h = _tile_w(np.ascontiguousarray(Wk[sl, :].T), HH, 128)
        wv_h = _tile_w(np.ascontiguousarray(Wv[sl, :].T), HH, 128)
        bq_h = np.ascontiguousarray(bq[sl].reshape(HH, 128).T)
        bk_h = np.ascontiguousarray(bk[sl].reshape(HH, 128).T)
        bv_h = np.ascontiguousarray(bv[sl].reshape(HH, 128).T)
        gidx_v = np.empty((128, 32), np.int32)
        for t in range(32):
            gidx_v[:, t] = (t * 128 + np.arange(128)) * 2 + r
        halves.append((wq_h, wk_h, wv_h, bq_h, bk_h, bv_h, gidx_v))

    wo_h = _tile_w(np.ascontiguousarray(Wo.T), 8, 512)
    gw_h = _tile_w(np.ascontiguousarray(gw.T), NIT, 128)
    uw_h = _tile_w(np.ascontiguousarray(uw.T), NIT, 128)
    dw_h = _tile_w(np.ascontiguousarray(dw.T), 32, 128)

    in_maps = []
    for cid in range(8):
        b = cid // 2
        r = cid % 2
        wq_h, wk_h, wv_h, bq_h, bk_h, bv_h, gidx_v = halves[r]
        in_maps.append({
            "xa": np.ascontiguousarray(x[b]),
            "xo": np.ascontiguousarray(x[b, MROWS * r:MROWS * (r + 1)]),
            "wq": wq_h, "wk": wk_h, "wv": wv_h, "wo": wo_h,
            "gwh": gw_h, "uwh": uw_h, "dwh": dw_h,
            "bqh": bq_h, "bkh": bk_h, "bvh": bv_h,
            "bo": bo_.reshape(1, D),
            "gsc": gs, "usc": us, "dsc": ds,
            "identw": ident, "identfw": identf, "maskw": mask,
            "gidx": gidx_v,
        })

    nc = build_nc()
    res = run_bass_kernel_spmd(nc, in_maps, core_ids=list(range(8)))
    out = np.empty((NB, SEQ, D), np.float32)
    for cid in range(8):
        b = cid // 2
        r = cid % 2
        out[b, MROWS * r:MROWS * (r + 1), :] = res.results[cid]["outp"]
    return out
